# version 34
# baseline (speedup 1.0000x reference)
from contextlib import ExitStack
import numpy as np
import ml_dtypes
import concourse.bass as bass
import concourse.mybir as mybir
from concourse.bass_utils import run_bass_kernel_spmd

F32 = mybir.dt.float32
BF16 = mybir.dt.bfloat16
AF = mybir.ActivationFunctionType
ALU = mybir.AluOpType
AX = mybir.AxisListType

NCORES = 8
L = 2048
D = 1024
C = 8
LT = 16
E = 8
EPS_ATTN = 1e-6
EPS_LN = 1e-5

SCR_WU = 0
SCR_K2 = 1088
SCR_EK = 3136
SCR_MSTG = 7232

AXD_CTX = 0
AXD_CTXT = 2048
AXD_ZF = 2048
AXD_DW = 8192

AXB_MALL = 0
AXB_H = 8192
AXB_MSEL = 10240
AXB_WDSEL = 11264
AXB_ROW = 11776

DBG = ()
SCHED = []


def build_program(dbg=()):
    dbg = set(dbg)
    nc = bass.Bass("TRN2", target_bir_lowering=False)

    def din(name, shape, dt=F32):
        return nc.dram_tensor(name, shape, dt, kind="ExternalInput").ap()

    xT = din("xT", [128, C * L])
    xtok = din("xtok", [L, D])
    xtok_t = xtok.rearrange("(t p) d -> t p d", p=128)
    wq = din("wq", [128, C * D])
    wk = din("wk", [128, C * D])
    wv = din("wv", [128, C * D])
    wo = din("wo", [128, C * D])
    wbase = din("wbase", [128, C * D])
    w0pair = din("w0pair", [128, C * 128])
    downw = din("downw", [128, C * E * 64])
    wuaug = din("wuaug", [128, E * C * 65])
    wct = din("wct", [128, E * C * D])
    smalls = din("smalls", [128, 152])
    rows3 = din("rows3", [1, 3 * D])
    convb = din("convb", [E, D])
    identb = din("identb", [128, 128])
    negmask = din("negmask", [128, 128])
    onesrow_b = din("onesrow_b", [1, 128])
    onescol_b = din("onescol_b", [128, 1])
    ones1f = din("ones1f", [1, 128], F32)
    onescolf = din("onescolf", [128, 1], F32)
    blockones2 = din("blockones2", [128, 2], F32)
    indic2 = din("indic2", [2, 128], F32)

    out = nc.dram_tensor("out", [L, D], F32, kind="ExternalOutput").ap()
    out_t = out.rearrange("(t p) d -> t p d", p=128)
    mall_dram = nc.dram_tensor("mall_scratch", [65, E * D], BF16).ap()

    with ExitStack() as ctx:
        def sb(name, shape, dt=BF16):
            return ctx.enter_context(nc.sbuf_tensor(name, shape, dt))

        def pst(name):
            return ctx.enter_context(nc.psum_tensor(name, [128, 512], F32))

        axA = sb("axA", [128, 16384])
        axB = sb("axB", [128, 16384])
        axC = sb("axC", [128, 16384])
        axD = sb("axD", [128, 16384])
        scr = sb("scr", [128, 8256])
        wg = sb("wg", [128, 16384])
        iostg = sb("iostg", [128, 2048], F32)
        xstg = sb("xstg", [128, 2048])
        ndb = sb("ndb", [128, C * 130])
        smallsb = sb("smallsb", [128, 152], F32)
        rows3b = sb("rows3b", [1, 3 * D], BF16)
        identb_s = sb("identb_s", [128, 128])
        negmask_s = sb("negmask_s", [128, 128])
        w0pb = sb("w0pb", [128, C * 128])
        onesrow_s = sb("onesrow_s", [1, 128])
        onescol_s = sb("onescol_s", [128, 1])
        ones1f_s = sb("ones1f_s", [1, 128], F32)
        onescolf_s = sb("onescolf_s", [128, 1], F32)
        blk2_s = sb("blk2_s", [128, 2], F32)
        indic2_s = sb("indic2_s", [2, 128], F32)
        convb_s = sb("convb_s", [E, D])
        bvbc = sb("bvbc", [128, D])
        discw = sb("discw", [128, C * 24], F32)
        mtmp = sb("mtmp", [128, C * 8], F32)
        mstat = sb("mstat", [128, C * 24], F32)
        mcols = sb("mcols", [128, C * 2], F32)
        lossv = sb("lossv", [1, 16], F32)
        mask8 = sb("mask8", [8, 1])
        maskbc = sb("maskbc", [128, 8], F32)
        invz = sb("invz", [2, 1], F32)
        dcol = sb("dcol", [128, 1], F32)
        bdcol = sb("bdcol", [64, 1], F32)
        lnst = sb("lnst", [128, 40], F32)
        ddbuf = sb("ddbuf", [128, 24], F32)
        epscol = sb("epscol", [128, 1], F32)
        mxb = sb("mxb", [128, 24], F32)

        axDf = axD.bitcast(F32)
        axBf = axB.bitcast(F32)

        pb = [pst(f"pb{i}") for i in range(8)]
        pbb = [p.bitcast(BF16) for p in pb]

        SEM = {}
        for k in ("pe", "sc", "ve", "sd", "gp", "g0", "ga", "gb",
                  "gc", "gw", "g1", "gx", "xq1", "xq2", "xq3", "g2", "s0",
                  "sx0", "sx1", "sx2", "sx3", "so0", "so1", "sm", "sm2", "sm3"):
            SEM[k] = ctx.enter_context(nc.semaphore("s_" + k))
        cnt = dict(pe=0, sc=0, ve=0, sd=0, gp=0, g0=0, ga=0, gb=0,
                   gc=0, gw=0, g1=0, gx=0, xq1=0, xq2=0, s0=0, sx0=0, sx1=0,
                   so0=0, so1=0, sm=0, sm2=0, sm3=0, xq3=0, g2=0, sx2=0, sx3=0)
        opsq = {k: [] for k in ("tensor", "scalar", "vector", "sync", "gpsimd")}

        def em(eng, fn):
            opsq[eng].append(fn)

        SCHED.clear()

        def wait(eng, key, v):
            if v and v > 0:
                SCHED.append(("wait", eng, key, v))
                em(eng, lambda e, s=SEM[key], v=v: e.wait_ge(s, v))

        def emi(eng, build, inckey=None, n=1):
            if inckey is None:
                SCHED.append(("op", eng, None, 0))
                em(eng, build)
                return None
            cnt[inckey] += n
            SCHED.append(("op", eng, inckey, n))
            em(eng, lambda e, b=build, s=SEM[inckey], n=n: b(e).then_inc(s, n))
            return cnt[inckey]



        def dbg_out(name, src_ap, gates):
            if name not in dbg:
                return
            ap = nc.dram_tensor("dbg_" + name, list(src_ap.shape), src_ap.dtype,
                                kind="ExternalOutput").ap()
            for k, v in gates:
                wait("sync", k, v)
            emi("sync", lambda s, a=ap, b=src_ap: s.dma_start(a, b), "sd", 16)

        for dst, src in ((ones1f_s, ones1f), (onescolf_s, onescolf),
                         (blk2_s, blockones2), (indic2_s, indic2),
                         (smallsb, smalls)):
            emi("sync", lambda s, d=dst, x=src: s.dma_start(d[:], x), "s0", 16)
        sd_consts = cnt["s0"]
        for dst, src in ((identb_s, identb), (negmask_s, negmask),
                         (onesrow_s, onesrow_b), (onescol_s, onescol_b)):
            emi("gpsimd", lambda g, d=dst, x=src: g.dma_start(d[:], x), "g0", 16)

        emi("gpsimd", lambda g: g.dma_start(wg[:, 0:4096], wk[:, 0:4096]), "g1", 16)
        emi("gpsimd", lambda g: g.dma_start(axA[:, 0:4096], xT[:, 0:4096]), "g1", 16)
        wait("gpsimd", "g1", cnt["g1"])
        emi("gpsimd", lambda g: g.dma_start(wg[:, 4096:8192], wk[:, 4096:8192]),
            "gx", 16)
        wait("gpsimd", "gx", cnt["gx"])
        emi("gpsimd", lambda g: g.dma_start(axA[:, 4096:8192], xT[:, 4096:8192]),
            "xq1", 16)
        wait("gpsimd", "xq1", cnt["xq1"])
        emi("gpsimd", lambda g: g.dma_start(axA[:, 8192:12288], xT[:, 8192:12288]),
            "xq2", 16)
        emi("gpsimd", lambda g: g.dma_start(axA[:, 12288:16384], xT[:, 12288:16384]),
            "xq3", 16)
        emi("gpsimd", lambda g: g.dma_start(wg[:, 8192:16384], wv), "gb", 16)
        emi("gpsimd", lambda g: g.dma_start(w0pb[:], w0pair), "g2", 16)
        emi("gpsimd", lambda g: g.dma_start(rows3b[:], rows3), "g2", 16)
        emi("gpsimd", lambda g: g.dma_start(convb_s[:], convb), "g2", 16)
        g0_all = cnt["g0"]
        g1_all = cnt["g1"]
        gx_all = cnt["gx"]
        g2_all = cnt["g2"]
        xq1_all = cnt["xq1"]
        xq2_all = cnt["xq2"]
        xq3_all = cnt["xq3"]
        gd_wv = cnt["gb"]

        from concourse.hw_specs import get_activation_tables
        nlx_idx = list(get_activation_tables(nc.m.arch).keys()).index(
            "natural_log_exp_and_others")

        def _preload(sc):
            return sc.add_instruction(mybir.InstLoadActFuncSet(
                name=nc.get_next_instruction_name(),
                act_func_set_id=nlx_idx, ins=[], outs=[]))
        em("scalar", _preload)

        ve_eps = emi("vector", lambda v: v.memset(epscol[:], EPS_LN), "ve")

        wait("vector", "s0", sd_consts)
        for step in range(6):
            if step:
                wait("vector", "ve", cnt["ve"])
            for c in range(C):
                mu = smallsb[:, 16 + c * 8:24 + c * 8]
                sg = smallsb[:, 80 + c * 8:88 + c * 8]
                ivd = discw[:, c * 24:c * 24 + 8]
                n2 = discw[:, c * 24 + 8:c * 24 + 16]
                m2w = discw[:, c * 24 + 16:c * 24 + 24]
                mi = mtmp[:, c * 8:c * 8 + 8]
                ik = "ve" if c == C - 1 else None
                if step == 0:
                    emi("vector", lambda v, o=ivd, a=sg: v.tensor_mul(o, a, a), ik)
                elif step == 1:
                    emi("vector", lambda v, o=ivd: v.reciprocal(o, o), ik)
                elif step == 2:
                    emi("vector", lambda v, o=mi, a=mu, b=ivd: v.tensor_mul(o, a, b),
                        ik)
                elif step == 3:
                    emi("vector", lambda v, o=m2w, a=mu, b=mi: v.tensor_mul(o, a, b),
                        ik)
                elif step == 4:
                    emi("vector", lambda v, o=n2, a=mi: v.tensor_scalar(
                        o, a, -2.0 / 1024.0, None, ALU.mult), ik)
                else:
                    emi("vector", lambda v, o=ivd, a=ivd: v.tensor_scalar(
                        o, a, 1.0 / 1024.0, None, ALU.mult), ik)
        wait("vector", "ve", cnt["ve"])
        for c in range(C):
            m2w = discw[:, c * 24 + 16:c * 24 + 24]
            emi("vector", lambda v, o=m2w, a=m2w: v.tensor_scalar(
                o, a, 1.0 / 1024.0, None, ALU.mult), "ve" if c == C - 1 else None)
        ve_disc = cnt["ve"]

        ve_ndbinit = emi("vector", lambda v: v.memset(ndb[:], 0.0), "ve")

        wait("tensor", "g1", g1_all)
        wait("tensor", "s0", sd_consts)
        wait("scalar", "s0", sd_consts)
        kmark = {}
        pe_last = 0
        for i in range(32):
            ltb, mc = divmod(i, 8)
            bank = pb[i % 2]
            if i - 2 in kmark:
                wait("tensor", "sc", kmark[i - 2])
            if i == 4:
                wait("tensor", "gx", gx_all)
            if i == 8:
                wait("tensor", "xq1", xq1_all)
            if i == 16:
                wait("tensor", "xq2", xq2_all)
            if i == 24:
                wait("tensor", "xq3", xq3_all)
            for kc in range(C):
                lhs = wg[:, mc * 1024 + kc * 128: mc * 1024 + kc * 128 + 128]
                rhs = axA[:, ltb * 4096 + kc * 512: ltb * 4096 + kc * 512 + 512]
                if kc == C - 1:
                    pe_last = emi("tensor", lambda t, o=bank[:, 0:512], a=lhs, b=rhs:
                                  t.matmul(o, a, b, start=False, stop=True), "pe")
                else:
                    emi("tensor", lambda t, o=bank[:, 0:512], a=lhs, b=rhs,
                        st=(kc == 0): t.matmul(o, a, b, start=st, stop=False))
            wait("scalar", "pe", pe_last)
            dst = axD[:, mc * 2048 + ltb * 512: mc * 2048 + ltb * 512 + 512]
            bap = smallsb[:, 8 + mc:9 + mc]
            kmark[i] = emi("scalar", lambda sc, o=dst, i_=bank[:, 0:512], b=bap:
                           sc.activation(o, i_, AF.Identity, bias=b, scale=1.0), "sc")
        pe_k_done = pe_last
        dbg_out("kTb", axD[:], [("sc", cnt["sc"])])

        wait("tensor", "g0", g0_all)
        wait("tensor", "g2", g2_all)
        emi("tensor", lambda t: t.matmul(pb[7][:, 0:512], onesrow_s[:],
                                         rows3b[0:1, 0:512], start=True, stop=True))
        pe_bv = emi("tensor", lambda t: t.matmul(pb[6][:, 0:512], onesrow_s[:],
                                                 rows3b[0:1, 512:1024],
                                                 start=True, stop=True), "pe")
        wait("vector", "pe", pe_bv)
        emi("vector", lambda v: v.tensor_copy(bvbc[:, 0:512], pb[7][:, 0:512]))
        ve_bvbc = emi("vector", lambda v: v.tensor_copy(bvbc[:, 512:1024],
                                                        pb[6][:, 0:512]), "ve")

        wait("gpsimd", "pe", pe_k_done)
        gd_wq = emi("gpsimd", lambda g: g.dma_start(wg[:, 0:8192], wq), "ga", 16)

        wait("tensor", "gb", gd_wv)
        vmark = {}
        for i in range(32):
            lt, oh = divmod(i, 2)
            bank = pb[2 + i % 2]
            if i - 2 in vmark:
                wait("tensor", "ve", vmark[i - 2])
            for kc in range(C):
                lhs = axA[:, (lt // 4) * 4096 + kc * 512 + (lt % 4) * 128:
                          (lt // 4) * 4096 + kc * 512 + (lt % 4) * 128 + 128]
                rhs = wg[:, 8192 + kc * 1024 + oh * 512:
                         8192 + kc * 1024 + oh * 512 + 512]
                if kc == C - 1:
                    pe_last = emi("tensor", lambda t, o=bank[:, 0:512], a=lhs, b=rhs:
                                  t.matmul(o, a, b, start=False, stop=True), "pe")
                else:
                    emi("tensor", lambda t, o=bank[:, 0:512], a=lhs, b=rhs,
                        st=(kc == 0): t.matmul(o, a, b, start=st, stop=False))
            wait("vector", "pe", pe_last)
            if i == 0:
                wait("vector", "ve", ve_bvbc)
            dst = axC[:, lt * 1024 + oh * 512: lt * 1024 + oh * 512 + 512]
            vmark[i] = emi("vector", lambda v, o=dst, a=bank[:, 0:512],
                           b=bvbc[:, oh * 512:oh * 512 + 512]:
                           v.tensor_add(o, a, b), "ve")
        pe_v_done = pe_last
        ve_v_done = cnt["ve"]
        dbg_out("vb", axC[:], [("ve", ve_v_done)])

        wait("gpsimd", "pe", pe_v_done)
        gd_wo = emi("gpsimd", lambda g: g.dma_start(wg[:, 8192:16384], wo), "gb", 16)

        wait("tensor", "ga", gd_wq)
        wait("tensor", "ve", ve_v_done)
        wait("tensor", "sc", kmark[31])
        qmark, sqsc, trve = {}, {}, {}

        def p3_q(i):
            c, ltb = divmod(i, 4)
            qbank = pb[i % 2]
            qslot = scr[:, (i % 2) * 512:(i % 2) * 512 + 512]
            if i - 2 in qmark:
                wait("tensor", "ve", qmark[i - 2])
            for kc in range(C):
                lhs = wg[:, kc * 1024 + c * 128: kc * 1024 + c * 128 + 128]
                rhs = axA[:, ltb * 4096 + kc * 512: ltb * 4096 + kc * 512 + 512]
                if kc == C - 1:
                    pe_last = emi("tensor", lambda t, o=qbank[:, 0:512], a=lhs, b=rhs:
                                  t.matmul(o, a, b, start=False, stop=True), "pe")
                else:
                    emi("tensor", lambda t, o=qbank[:, 0:512], a=lhs, b=rhs,
                        st=(kc == 0): t.matmul(o, a, b, start=st, stop=False))
            wait("vector", "pe", pe_last)
            bap = smallsb[:, c:c + 1]
            qmark[i] = emi("vector", lambda v, o=qslot, a=qbank[:, 0:512], b=bap:
                           v.tensor_scalar(o, a, b, None, ALU.add), "ve")

        def p3_sq(i):
            c, ltb = divmod(i, 4)
            sbank = pb[2 + i % 2]
            qslot = scr[:, (i % 2) * 512:(i % 2) * 512 + 512]
            estage = scr[:, SCR_EK + (i % 3) * 512: SCR_EK + (i % 3) * 512 + 512]
            mxs = mxb[:, (i % 3) * 8:(i % 3) * 8 + 8]
            if i - 2 in sqsc:
                wait("tensor", "sc", sqsc[i - 2])
            wait("tensor", "ve", qmark[i])
            for j in range(4):
                oc = sbank[:, j * 128:j * 128 + 128]
                lhs = qslot[:, j * 128:j * 128 + 128]
                if j == 3:
                    pe_sq = emi("tensor", lambda t, o=oc, a=lhs,
                                b=w0pb[:, c * 128:c * 128 + 128]:
                                t.matmul(o, a, b, start=True, stop=True), "pe")
                else:
                    emi("tensor", lambda t, o=oc, a=lhs,
                        b=w0pb[:, c * 128:c * 128 + 128]:
                        t.matmul(o, a, b, start=True, stop=True))
            wait("vector", "pe", pe_sq)
            s3 = sbank[:].rearrange("p (j h f) -> p j h f", j=4, h=2)
            emi("vector", lambda v, o=mxs[:, 0:4], a=s3[:, :, 0, :]:
                v.tensor_reduce(o, a, AX.X, ALU.max, negate=True))
            ve_mx = emi("vector", lambda v, o=mxs[:, 4:8], a=s3[:, :, 1, :]:
                        v.tensor_reduce(o, a, AX.X, ALU.max, negate=True), "ve")
            wait("scalar", "ve", ve_mx)
            for j in range(4):
                emi("scalar", lambda sc, o=estage[:, j * 128:j * 128 + 64],
                    i_=sbank[:, j * 128:j * 128 + 64], b=mxs[:, j:j + 1]:
                    sc.activation(o, i_, AF.Exp, bias=b, scale=1.0))
                sqsc[i] = emi("scalar", lambda sc,
                              o=estage[:, j * 128 + 64:j * 128 + 128],
                              i_=sbank[:, j * 128 + 64:j * 128 + 128],
                              b=mxs[:, 4 + j:5 + j]:
                              sc.activation(o, i_, AF.Exp, bias=b, scale=1.0), "sc")

        def p3_tr(i):
            c, ltb = divmod(i, 4)
            tbank = pbb[6 + i % 2]
            estage = scr[:, SCR_EK + (i % 3) * 512: SCR_EK + (i % 3) * 512 + 512]
            wait("tensor", "sc", sqsc[i])
            if i - 2 in trve:
                wait("tensor", "ve", trve[i - 2])
            for j in range(4):
                if j == 3:
                    pe_tr = emi("tensor", lambda t,
                                o=tbank[:, j * 128:j * 128 + 128],
                                a=estage[:, j * 128:j * 128 + 128]:
                                t.transpose(o, a, identb_s[:]), "pe")
                else:
                    emi("tensor", lambda t, o=tbank[:, j * 128:j * 128 + 128],
                        a=estage[:, j * 128:j * 128 + 128]:
                        t.transpose(o, a, identb_s[:]))
            wait("vector", "pe", pe_tr)
            trve[i] = emi("vector", lambda v,
                          o=axB[:, c * 2048 + ltb * 512:c * 2048 + ltb * 512 + 512],
                          a=tbank[:, 0:512]: v.tensor_copy(o, a), "ve")

        for s in range(34):
            if s < 32:
                p3_q(s)
            if 1 <= s <= 32:
                p3_sq(s - 1)
            if s >= 2:
                p3_tr(s - 2)
        pe_q_done = cnt["pe"]
        sc_eq_done = cnt["sc"]
        ve_eq_done = cnt["ve"]
        dbg_out("EqT", axB[:], [("ve", ve_eq_done)])

        wct_gd, wu_gd, me_pe, me_sc, me_sd = {}, {}, {}, {}, {}

        def sched_wct_dma(e, gate_pe):
            if gate_pe:
                wait("gpsimd", "pe", gate_pe)
            wct_gd[e] = emi("gpsimd", lambda g, e=e: g.dma_start(
                wg[:, 0:8192], wct[:, e * 8192:(e + 1) * 8192]), "gc", 16)
            wu_gd[e] = emi("gpsimd", lambda g, e=e: g.dma_start(
                scr[:, SCR_WU + (e % 2) * 520: SCR_WU + (e % 2) * 520 + 520],
                wuaug[:, e * 520:(e + 1) * 520]), "gw", 16)

        def emit_me(e):
            wait("tensor", "gc", wct_gd[e])
            wait("tensor", "gw", wu_gd[e])
            for half in range(2):
                if half == 1 or e >= 1:
                    wait("tensor", "sc", me_sc.get(e, me_sc.get(e - 1)))
                for kc in range(C):
                    lhs = scr[:, SCR_WU + (e % 2) * 520 + kc * 65:
                              SCR_WU + (e % 2) * 520 + kc * 65 + 65]
                    rhs = wg[:, kc * 1024 + half * 512:
                             kc * 1024 + half * 512 + 512]
                    if kc == C - 1:
                        me_pe[e] = emi("tensor", lambda t, o=pb[7][0:65, 0:512],
                                       a=lhs, b=rhs:
                                       t.matmul(o, a, b, start=False, stop=True), "pe")
                    else:
                        emi("tensor", lambda t, o=pb[7][0:65, 0:512], a=lhs, b=rhs,
                            st=(kc == 0): t.matmul(o, a, b, start=st, stop=False))
                wait("scalar", "pe", me_pe[e])
                if half == 0 and e >= 1:
                    wait("scalar", "sm", me_sd[e - 1])
                me_sc[e] = emi("scalar", lambda sc, half=half: sc.copy(
                    scr[0:65, SCR_MSTG + half * 512:SCR_MSTG + half * 512 + 512],
                    pb[7][0:65, 0:512]), "sc")
            wait("sync", "sc", me_sc[e])
            me_sd[e] = emi("sync", lambda s, e=e: s.dma_start(
                mall_dram[:, e * D:(e + 1) * D],
                scr[0:65, SCR_MSTG:SCR_MSTG + 1024]), "sm", 16)
            if e + 1 < E:
                sched_wct_dma(e + 1, me_pe[e])

        def emit_me_next():
            done = set(me_pe)
            for e in range(E):
                if e in wct_gd and e not in done:
                    emit_me(e)
                    return

        sched_wct_dma(0, pe_q_done)

        wait("tensor", "ve", ve_ndbinit)
        wait("tensor", "ve", ve_eq_done)
        wait("tensor", "sc", sc_eq_done)
        ek_sc, nd_ve, ekpe, npe, k2ve = {}, {}, {}, {}, {}

        def p4_dcol(c):
            nbank = pb[2 + c % 2]
            wait("vector", "pe", npe[c])
            zv["d", c] = emi("vector", lambda v, a=nbank[:, 128:129]:
                             v.tensor_copy(dcol[:], a), "ve")

        def p4_zpair(c):
            wait("tensor", "ve", zv["d", c])
            zv["zp", c] = emi("tensor", lambda t: t.matmul(
                pb[5][0:2, 0:1], blk2_s[:], dcol[:], start=True, stop=True), "pe")

        def p4_recip(c):
            wait("vector", "pe", zv["zp", c])
            zv["iz", c] = emi("vector", lambda v: v.reciprocal(
                invz[:], pb[5][0:2, 0:1]), "ve")

        def p4_bc(c):
            wait("tensor", "ve", zv["iz", c])
            zv["bc", c] = emi("tensor", lambda t: t.matmul(
                pb[6][:, 0:1], indic2_s[:], invz[:], start=True, stop=True), "pe")

        def p4_ndb_evac(c):
            nbank = pb[2 + c % 2]
            wait("vector", "pe", zv["bc", c])
            emi("vector", lambda v, c=c, nb=nbank: v.tensor_scalar(
                ndb[0:64, c * 130:c * 130 + 64], nb[0:64, 0:64],
                pb[6][0:64, 0:1], None, ALU.mult))
            emi("vector", lambda v, c=c, nb=nbank: v.tensor_scalar(
                ndb[64:128, c * 130 + 65:c * 130 + 129], nb[64:128, 64:128],
                pb[6][64:128, 0:1], None, ALU.mult))
            emi("vector", lambda v, c=c, nb=nbank: v.tensor_scalar(
                ndb[0:64, c * 130 + 64:c * 130 + 65], nb[0:64, 128:129],
                pb[6][0:64, 0:1], EPS_ATTN, ALU.mult, ALU.add))
            nd_ve[c] = emi("vector", lambda v, c=c, nb=nbank: v.tensor_scalar(
                ndb[64:128, c * 130 + 129:c * 130 + 130], nb[64:128, 128:129],
                pb[6][64:128, 0:1], EPS_ATTN, ALU.mult, ALU.add), "ve")

        zv = {}
        k2lb = {}
        for c in range(C):
            if c - 1 in ekpe:
                wait("vector", "pe", ekpe[c - 1])
            for lb in range(4):
                src2 = axD[:, c * 2048 + lb * 512: c * 2048 + lb * 512 + 512]
                dstk = scr[:, SCR_K2 + lb * 512: SCR_K2 + lb * 512 + 512]
                k2lb[(c, lb)] = emi("vector", lambda v, o=dstk, a=src2:
                                    v.tensor_mul(o, a, a), "ve")
            if c >= 1:
                p4_dcol(c - 1)

            for lb in range(4):
                bank = pb[lb % 2]
                prev = ek_sc.get((c, lb - 2)) or ek_sc.get((c - 1, lb + 2))
                if prev:
                    wait("tensor", "sc", prev)
                wait("tensor", "ve", k2lb[(c, lb)])
                pe_ek = 0
                for j in range(4):
                    lt = lb * 4 + j
                    lhs1 = axD[:, c * 2048 + lt * 128: c * 2048 + lt * 128 + 128]
                    lhs2 = scr[:, SCR_K2 + lt * 128: SCR_K2 + lt * 128 + 128]
                    oc = bank[:, j * 128:j * 128 + 128]
                    emi("tensor", lambda t, o=oc, a=lhs1,
                        b=w0pb[:, c * 128:c * 128 + 128]:
                        t.matmul(o, a, b, start=True, stop=False))
                    if j == 3:
                        pe_ek = emi("tensor", lambda t, o=oc, a=lhs2, b=negmask_s[:]:
                                    t.matmul(o, a, b, start=False, stop=True), "pe")
                    else:
                        emi("tensor", lambda t, o=oc, a=lhs2, b=negmask_s[:]:
                            t.matmul(o, a, b, start=False, stop=True))
                ekpe[c] = pe_ek
                if lb == 0 and c >= 1:
                    p4_zpair(c - 1)
                    p4_recip(c - 1)
                wait("scalar", "pe", pe_ek)
                edst = scr[:, SCR_EK + (c % 2) * 2048 + lb * 512:
                           SCR_EK + (c % 2) * 2048 + lb * 512 + 512]
                ek_sc[(c, lb)] = emi("scalar", lambda sc, o=edst, i_=bank[:, 0:512]:
                                     sc.activation(o, i_, AF.Exp, bias=0.0,
                                                   scale=1.0), "sc")

            nbank = pb[2 + c % 2]
            if c - 2 in nd_ve:
                wait("tensor", "ve", nd_ve[c - 2])
            first = True
            for lb in range(4):
                wait("tensor", "sc", ek_sc[(c, lb)])
                for j in range(4):
                    lt = lb * 4 + j
                    lhs = scr[:, SCR_EK + (c % 2) * 2048 + lt * 128:
                              SCR_EK + (c % 2) * 2048 + lt * 128 + 128]
                    rhs = axC[:, lt * 1024 + c * 128: lt * 1024 + c * 128 + 128]
                    emi("tensor", lambda t, o=nbank[:, 0:128], a=lhs, b=rhs,
                        st=first: t.matmul(o, a, b, start=st, stop=False))
                    if lb == 3 and j == 3:
                        npe[c] = emi("tensor", lambda t, o=nbank[:, 128:129], a=lhs,
                                     b=onescol_s[:]:
                                     t.matmul(o, a, b, start=False, stop=True), "pe")
                    else:
                        emi("tensor", lambda t, o=nbank[:, 128:129], a=lhs,
                            b=onescol_s[:]:
                            t.matmul(o, a, b, start=False, stop=False))
                    first = False
            if c >= 1:
                p4_bc(c - 1)
                p4_ndb_evac(c - 1)
            if c in (3, 6):
                emit_me_next()
        p4_dcol(C - 1)
        p4_zpair(C - 1)
        p4_recip(C - 1)
        p4_bc(C - 1)
        p4_ndb_evac(C - 1)
        ve_nd_all = cnt["ve"]
        dbg_out("ndb", ndb[:], [("ve", ve_nd_all)])

        wait("gpsimd", "pe", ekpe[C - 1])
        gd_dw = emi("gpsimd", lambda g: g.dma_start(
            axD[:, AXD_DW:AXD_DW + 4096], downw), "ga", 16)

        wait("tensor", "gb", gd_wo)

        def xslot(lt):
            if lt % 4 < 2:
                return xstg[:, (lt % 4) * 1024:(lt % 4) * 1024 + 1024]
            return scr[:, SCR_K2 + (lt % 4 - 2) * 1024:
                       SCR_K2 + (lt % 4 - 2) * 1024 + 1024]

        XSEM = ("sx0", "sx1", "sx2", "sx3")
        xsd = {}
        for lt0 in range(4):
            xsd[lt0] = emi("gpsimd", lambda g, lt=lt0: g.dma_start(
                xslot(lt), xtok_t[lt]), XSEM[lt0 % 4], 16)

        num_sc, ctxt_ve, x2t_ve, x2_sc, wo_pe, trb_pe = {}, {}, {}, {}, {}, {}
        cw_sc = {}
        p5mk = {}
        mpend, mq_mark = [], {}
        num_grp = {}

        AGRP = ((0, 3), (3, 6), (6, 8))

        def p5_A(lt):
            for g, (c0, c1) in enumerate(AGRP):
                G = lt * 3 + g
                bank = pb[G % 3]
                if lt == 0:
                    wait("tensor", "ve", nd_ve[c1 - 1])
                if G - 3 in num_grp:
                    wait("tensor", "sc", num_grp[G - 3])
                pe_num = 0
                for c in range(c0, c1):
                    lhs = axB[:, c * 2048 + lt * 128: c * 2048 + lt * 128 + 128]
                    pe_num = emi("tensor", lambda t,
                                 o=bank[:, (c - c0) * 130:(c - c0) * 130 + 130],
                                 a=lhs, b=ndb[:, c * 130:(c + 1) * 130]:
                                 t.matmul(o, a, b, start=True, stop=True), "pe")
                if G == LT * 3 - 1:
                    p5mk["numdone"] = pe_num
                wait("vector", "pe", pe_num)
                if G - 4 in num_grp:
                    wait("vector", "sc", num_grp[G - 4])
                n = c1 - c0
                b3 = bank[:, 0:n * 130].rearrange("p (c g x) -> p c g x", c=n, g=2)
                rbase = (G % 4) * 6
                rs3 = ddbuf[:, rbase:rbase + n * 2].rearrange(
                    "p (c g) -> p c g", c=n)
                ve_r = emi("vector", lambda v, o=rs3, a=b3[:, :, :, 64]:
                           v.reciprocal(o, a), "ve")
                wait("scalar", "ve", ve_r)
                if lt - 2 in trb_pe and g == 0:
                    wait("scalar", "pe", trb_pe[lt - 2])
                for c in range(c0, c1):
                    i = lt * C + c
                    off = (c - c0) * 130
                    cbase = AXD_CTX + (lt % 2) * 1024 + c * 128
                    emi("scalar", lambda sc, o=axD[:, cbase:cbase + 64],
                        i_=bank[:, off:off + 64],
                        r=ddbuf[:, rbase + (c - c0) * 2:rbase + (c - c0) * 2 + 1]:
                        sc.activation(o, i_, AF.Identity, bias=0.0, scale=r))
                    num_sc[i] = emi("scalar", lambda sc,
                                    o=axD[:, cbase + 64:cbase + 128],
                                    i_=bank[:, off + 65:off + 129],
                                    r=ddbuf[:, rbase + (c - c0) * 2 + 1:
                                            rbase + (c - c0) * 2 + 2]:
                                    sc.activation(o, i_, AF.Identity, bias=0.0,
                                                  scale=r), "sc")
                num_grp[G] = num_sc[lt * C + c1 - 1]

        def p5_B_mm(lt):
            if lt - 1 in ctxt_ve:
                wait("tensor", "ve", ctxt_ve[lt - 1])
            for c in range(C):
                wait("tensor", "sc", num_sc[lt * C + c])
                src = axD[:, AXD_CTX + (lt % 2) * 1024 + c * 128:
                          AXD_CTX + (lt % 2) * 1024 + c * 128 + 128]
                if c == C - 1:
                    trb_pe[lt] = emi("tensor",
                                     lambda t, o=pbb[3][:, c * 128:c * 128 + 128],
                                     a=src: t.transpose(o, a, identb_s[:]), "pe")
                else:
                    emi("tensor", lambda t, o=pbb[3][:, c * 128:c * 128 + 128],
                        a=src: t.transpose(o, a, identb_s[:]))

        def p5_B_evac(lt):
            wait("vector", "pe", trb_pe[lt])
            if lt - 2 in wo_pe:
                wait("vector", "pe", wo_pe[lt - 2])
            ctxt_ve[lt] = emi("vector", lambda v,
                              o=axD[:, AXD_CTXT + (lt % 2) * 1024:
                                    AXD_CTXT + (lt % 2) * 1024 + 1024],
                              a=pbb[3][:, 0:1024]: v.tensor_copy(o, a), "ve")

        def p5_C_mm(lt):
            wait("tensor", "ve", ctxt_ve[lt])
            mark = 0
            if lt - 1 in cw_sc:
                wait("tensor", "ve", cw_sc[lt - 1])
            for oh in range(2):
                bank = pb[4 + oh]
                for kc in range(C):
                    lhs = axD[:, AXD_CTXT + (lt % 2) * 1024 + kc * 128:
                              AXD_CTXT + (lt % 2) * 1024 + kc * 128 + 128]
                    rhs = wg[:, 8192 + kc * 1024 + oh * 512:
                             8192 + kc * 1024 + oh * 512 + 512]
                    emi("tensor", lambda t, o=bank[:, 0:512], a=lhs, b=rhs,
                        st=(kc == 0): t.matmul(o, a, b, start=st, stop=False))
                mark = emi("tensor", lambda t, o=bank[:, 0:512], a=onesrow_s[:],
                           b=rows3b[0:1, 1024 + oh * 512:1024 + oh * 512 + 512]:
                           t.matmul(o, a, b, start=False, stop=True), "pe")
            wo_pe[lt] = mark

        def p5_CW(lt):
            stg = iostg[:, (lt % 2) * 1024:(lt % 2) * 1024 + 1024]
            xsl = xslot(lt)
            wait("vector", "pe", wo_pe[lt])
            wait("vector", XSEM[lt % 4], xsd[lt])
            if lt - 2 in x2_sc:
                wait("vector", "sc", x2_sc[lt - 2])
            emi("vector", lambda v, o=stg[:, 0:512], a=pb[4][:, 0:512],
                b=xsl[:, 0:512]: v.tensor_add(o, a, b))
            cw_sc[lt] = emi("vector", lambda v, o=stg[:, 512:1024],
                            a=pb[5][:, 0:512], b=xsl[:, 512:1024]:
                            v.tensor_add(o, a, b), "ve")
            if lt + 4 < LT:
                wait("gpsimd", "ve", cw_sc[lt])
                xsd[lt + 4] = emi("gpsimd", lambda g, lt2=lt + 4: g.dma_start(
                    xslot(lt2), xtok_t[lt2]), XSEM[lt % 4], 16)

        def p5_C_ln(lt):
            ls = lnst[:, (lt % 2) * 20:(lt % 2) * 20 + 20]
            nmr = ls[:, 16:17]
            rstd = ls[:, 15:16]
            stg = iostg[:, (lt % 2) * 1024:(lt % 2) * 1024 + 1024]
            wait("vector", "ve", cw_sc[lt])
            emi("vector", lambda v, o=ls[:, 0:6], a=stg[:, 0:512]:
                v.bn_stats(o, a))
            emi("vector", lambda v, o=ls[:, 6:12], a=stg[:, 512:1024]:
                v.bn_stats(o, a), "ve")
            wait("vector", "ve", cnt["ve"])
            ve_bn = emi("vector", lambda v, o=ls[:, 12:14], a=ls[:, 0:12]:
                        v.bn_aggr(o, a), "ve")
            wait("scalar", "ve", ve_bn)
            emi("scalar", lambda sc, o=ls[:, 14:15], a=ls[:, 13:14]:
                sc.activation(o, a, AF.Ln, bias=epscol[:], scale=1.0), "sc")
            wait("scalar", "sc", cnt["sc"])
            sc_r = emi("scalar", lambda sc, o=rstd, a=ls[:, 14:15]:
                       sc.activation(o, a, AF.Exp, bias=0.0, scale=-0.5), "sc")
            wait("vector", "sc", sc_r)
            ve_nmr = emi("vector", lambda v, o=nmr, a=ls[:, 12:13], b=rstd:
                         v.scalar_tensor_tensor(o, a, -1.0, b, ALU.mult, ALU.mult),
                         "ve")
            wait("scalar", "ve", ve_nmr)
            emi("scalar", lambda sc, o=axA[:, lt * 1024:lt * 1024 + 512],
                a=stg[:, 0:512], b=nmr, r=rstd:
                sc.activation(o, a, AF.Identity, bias=b, scale=r), "sc")
            x2_sc[lt] = emi("scalar", lambda sc,
                            o=axA[:, lt * 1024 + 512:lt * 1024 + 1024],
                            a=stg[:, 512:1024], b=nmr, r=rstd:
                            sc.activation(o, a, AF.Identity, bias=b, scale=r), "sc")

        def p5_D(lt):
            wait("tensor", "sc", x2_sc[lt])
            if lt - 1 in x2t_ve:
                wait("tensor", "ve", x2t_ve[lt - 1])
            for c in range(C):
                src = axA[:, lt * 1024 + c * 128: lt * 1024 + c * 128 + 128]
                if c == C - 1:
                    pe_x2 = emi("tensor",
                                lambda t, o=pbb[6][:, c * 128:c * 128 + 128],
                                a=src: t.transpose(o, a, identb_s[:]), "pe")
                else:
                    emi("tensor", lambda t, o=pbb[6][:, c * 128:c * 128 + 128],
                        a=src: t.transpose(o, a, identb_s[:]))
            wait("vector", "pe", pe_x2)
            dst3 = axC[:].rearrange("p (c l) -> p c l", c=C)[:, :,
                                                            lt * 128:lt * 128 + 128]
            src3 = pbb[6][:].rearrange("p (c l) -> p c l", c=C)
            x2t_ve[lt] = emi("vector", lambda v, o=dst3, a=src3:
                             v.tensor_copy(o, a), "ve")
            if lt % 4 == 3:
                q4 = lt // 4
                mq_mark[q4] = x2t_ve[lt]
                mpend.extend((c, q4) for c in range(C))
            for _ in range(2):
                if not mpend:
                    break
                c, q = mpend.pop(0)
                wait("vector", "ve", mq_mark[q])
                emi("vector", lambda v, c=c, q=q: v.bn_stats(
                    mstat[:, c * 24 + q * 6:c * 24 + q * 6 + 6],
                    axC[:, c * 2048 + q * 512:c * 2048 + q * 512 + 512]), "ve")

        for s in range(LT + 4):
            if s < LT:
                p5_A(s)
            if 1 <= s <= LT:
                p5_B_mm(s - 1)
                p5_B_evac(s - 1)
            if 2 <= s <= LT + 1:
                p5_C_mm(s - 2)
            if 3 <= s <= LT + 2:
                p5_C_ln(s - 3)
            if 2 <= s <= LT + 1:
                p5_CW(s - 2)
            if 4 <= s <= LT + 3:
                p5_D(s - 4)
            if s in (2, 4, 6, 8, 10, 12, 14, 16):
                emit_me_next()
        ve_x2t_done = cnt["ve"]
        sc_x2_done = cnt["sc"]
        pe_p5_done = cnt["pe"]
        dbg_out("x2b", axA[:], [("sc", sc_x2_done)])
        dbg_out("x2Tb", axC[:], [("ve", ve_x2t_done)])
        for e in range(E):
            if e not in me_pe:
                emit_me(e)

        wait("gpsimd", "pe", me_pe[E - 1])
        gd_base = emi("gpsimd", lambda g: g.dma_start(wg[:, 0:8192], wbase), "gb", 16)

        wait("sync", "pe", p5mk["numdone"])
        wait("sync", "sm", me_sd[E - 2])
        sd_mall_a = emi("sync", lambda s: s.dma_start(
            axB[0:65, AXB_MALL:AXB_MALL + 7 * D], mall_dram[:, 0:7 * D]), "sm2", 16)
        wait("sync", "sm", me_sd[E - 1])
        sd_mall = emi("sync", lambda s: s.dma_start(
            axB[0:65, AXB_MALL + 7 * D:AXB_MALL + 8 * D],
            mall_dram[:, 7 * D:8 * D]), "sm3", 16)

        wait("vector", "sc", sc_x2_done)
        wait("vector", "pe", pe_p5_done)
        for c, q in mpend:
            wait("vector", "ve", mq_mark[q])
            emi("vector", lambda v, c=c, q=q: v.bn_stats(
                mstat[:, c * 24 + q * 6:c * 24 + q * 6 + 6],
                axC[:, c * 2048 + q * 512:c * 2048 + q * 512 + 512]), "ve")
        mpend.clear()
        wait("vector", "ve", cnt["ve"])
        for c in range(C):
            emi("vector", lambda v, c=c: v.bn_aggr(
                mcols[:, c * 2:c * 2 + 2], mstat[:, c * 24:c * 24 + 24]),
                "ve" if c == C - 1 else None)
        wait("vector", "ve", cnt["ve"])
        for c in range(C):
            emi("vector", lambda v, c=c: v.tensor_mul(
                mtmp[:, c * 8:c * 8 + 1], mcols[:, c * 2:c * 2 + 1],
                mcols[:, c * 2:c * 2 + 1]), "ve" if c == C - 1 else None)
        wait("vector", "ve", cnt["ve"])
        for c in range(C):
            emi("vector", lambda v, c=c: v.tensor_add(
                mcols[:, c * 2 + 1:c * 2 + 2], mcols[:, c * 2 + 1:c * 2 + 2],
                mtmp[:, c * 8:c * 8 + 1]), "ve" if c == C - 1 else None)
        ve_m = cnt["ve"]
        dbg_out("mcols", mcols[:], [("ve", ve_m)])

        wait("tensor", "gb", gd_base)
        wait("tensor", "sc", sc_x2_done)
        wait("tensor", "ve", ve_x2t_done)
        for oh in range(2):
            bank = pb[oh]
            for kc in range(C):
                lhs = axC[:, kc * 2048: kc * 2048 + 128]
                rhs = wg[:, kc * 1024 + oh * 512: kc * 1024 + oh * 512 + 512]
                emi("tensor", lambda t, o=bank[:, 0:512], a=lhs, b=rhs,
                    st=(kc == 0): t.matmul(o, a, b, start=st, stop=False))

        wait("tensor", "ve", ve_m)
        pe_loss = 0
        for c in range(C):
            emi("tensor", lambda t, a=mcols[:, c * 2 + 1:c * 2 + 2],
                b=discw[:, c * 24:c * 24 + 8], st=(c == 0):
                t.matmul(pb[2][0:1, 0:8], a, b, start=st, stop=False))
            emi("tensor", lambda t, a=mcols[:, c * 2:c * 2 + 1],
                b=discw[:, c * 24 + 8:c * 24 + 16]:
                t.matmul(pb[2][0:1, 0:8], a, b, start=False, stop=False))
            pe_loss = emi("tensor", lambda t, b=discw[:, c * 24 + 16:c * 24 + 24],
                          sp=(c == C - 1):
                          t.matmul(pb[2][0:1, 0:8], onescolf_s[:], b,
                                   start=False, stop=sp), "pe")
        dbg_rawloss = pe_loss

        wait("vector", "pe", pe_loss)
        emi("vector", lambda v: v.tensor_reduce(lossv[0:1, 8:9], pb[2][0:1, 0:8],
                                                AX.X, ALU.min), "ve")
        wait("vector", "ve", cnt["ve"])
        ve_mask = emi("vector", lambda v: v.tensor_scalar(
            lossv[0:1, 0:8], pb[2][0:1, 0:8], lossv[0:1, 8:9], None,
            ALU.is_equal), "ve")
        dbg_out("lossv", lossv[:], [("ve", ve_mask)])
        wait("tensor", "ve", ve_mask)
        pe_mt = emi("tensor", lambda t: t.transpose(pb[5][0:8, 0:1],
                                                    lossv[0:1, 0:8],
                                                    ones1f_s[0:1, 0:1]), "pe")
        pe_mb = emi("tensor", lambda t: t.matmul(pb[6][:, 0:8], ones1f_s[:],
                                                 lossv[0:1, 0:8],
                                                 start=True, stop=True), "pe")
        wait("vector", "pe", pe_mb)
        emi("vector", lambda v: v.tensor_copy(maskbc[:], pb[6][:, 0:8]), "ve")
        ve_mbc = emi("vector", lambda v: v.tensor_copy(mask8[:], pb[5][0:8, 0:1]),
                     "ve")
        dbg_out("maskbc", maskbc[:], [("ve", ve_mbc)])

        wait("tensor", "ve", ve_mbc)
        wait("tensor", "g0", g0_all)
        emi("tensor", lambda t: t.matmul(pb[5][0:1, 0:512], mask8[:],
                                         convb_s[:, 0:512], start=True, stop=True))
        pe_bc2 = emi("tensor", lambda t: t.matmul(pb[6][0:1, 0:512], mask8[:],
                                                  convb_s[:, 512:1024],
                                                  start=True, stop=True), "pe")

        wait("vector", "ve", ve_mbc)
        mid128 = scr
        ve_mid_e = {}
        for e in range(E):
            ve_mid_e[e] = emi("vector", lambda v, e=e: v.tensor_scalar(
                mid128[:, e * 128:e * 128 + 128], identb_s[:],
                maskbc[:, e:e + 1], None, ALU.mult), "ve")
        ve_mid = cnt["ve"]
        emi("vector", lambda v: v.tensor_mul(
            mtmp[0:64, 0:8], smallsb[0:64, 144:152], maskbc[0:64, 0:8]), "ve")
        wait("vector", "ve", cnt["ve"])
        ve_bd = emi("vector", lambda v: v.tensor_reduce(
            bdcol[:], mtmp[0:64, 0:8], AX.X, ALU.add), "ve")

        wait("tensor", "ga", gd_dw)
        wait("tensor", "sc", sc_x2_done)
        for c in range(C):
            for e in range(E):
                if c == 0:
                    wait("tensor", "ve", ve_mid_e[e])
                dwe = axD[:, AXD_DW + c * 512 + e * 64:AXD_DW + c * 512 + e * 64 + 64]
                emi("tensor", lambda t, o=pb[3][:, c * 64:c * 64 + 64],
                    a=mid128[:, e * 128:e * 128 + 128], b=dwe,
                    st=(e == 0), sp=(e == E - 1):
                    t.matmul(o, a, b, start=st, stop=sp),
                    "pe" if (c == C - 1 and e == E - 1) else None)
        pe_wd = cnt["pe"]
        wait("tensor", "sm2", sd_mall_a)
        msel_pe = {}
        wait("tensor", "sc", me_sc[E - 1])
        for oh in range(2):
            bank = pb[4] if oh == 0 else pb[7]
            for e in range(E):
                if oh == 0 and e == E - 1:
                    wait("tensor", "sm3", sd_mall)
                mall_e = axB[0:65, AXB_MALL + e * 1024 + oh * 512:
                             AXB_MALL + e * 1024 + oh * 512 + 512]
                emi("tensor", lambda t, o=bank[0:65, 0:512],
                    a=mid128[0:65, e * 128:e * 128 + 65], b=mall_e,
                    st=(e == 0), sp=(e == E - 1):
                    t.matmul(o, a, b, start=st, stop=sp),
                    "pe" if e == E - 1 else None)
            msel_pe[oh] = cnt["pe"]
        wait("scalar", "pe", pe_wd)
        sc_wd = emi("scalar", lambda sc: sc.activation(
            axB[:, AXB_WDSEL:AXB_WDSEL + 512], pb[3][:, 0:512],
            AF.Identity, bias=0.0, scale=1.0), "sc")
        wait("scalar", "pe", msel_pe[1])
        emi("scalar", lambda sc: sc.activation(
            axB[0:65, AXB_MSEL:AXB_MSEL + 512], pb[4][0:65, 0:512],
            AF.Identity, bias=0.0, scale=1.0), "sc")
        sc_msel = emi("scalar", lambda sc: sc.activation(
            axB[0:65, AXB_MSEL + 512:AXB_MSEL + 1024], pb[7][0:65, 0:512],
            AF.Identity, bias=0.0, scale=1.0), "sc")
        dbg_out("wdsel", axB[:, AXB_WDSEL:AXB_WDSEL + 512], [("sc", sc_wd)])

        rowstg = axB[0:1, AXB_ROW:AXB_ROW + 1024]
        wait("vector", "pe", pe_bc2)
        emi("vector", lambda v: v.tensor_add(rowstg[0:1, 0:512], pb[5][0:1, 0:512],
                                             rows3b[0:1, 2048:2560]))
        ve_br = emi("vector", lambda v: v.tensor_add(rowstg[0:1, 512:1024],
                                             pb[6][0:1, 0:512],
                                             rows3b[0:1, 2560:3072]), "ve")
        ve_hone = emi("vector", lambda v: v.memset(
            axB[64:65, AXB_H:AXB_H + 2048], 1.0), "ve")
        dbg_out("msel", axB[0:65, AXB_MSEL:AXB_MSEL + 1024], [("ve", ve_br)])

        wait("tensor", "sc", sc_wd)
        wait("tensor", "sc", me_sc[E - 1])
        wait("scalar", "ve", ve_bd)
        h_sc = {}
        for ltb in range(4):
            if ltb - 1 in h_sc:
                wait("tensor", "sc", h_sc[ltb - 1])
            for kc in range(C):
                lhs = axB[:, AXB_WDSEL + kc * 64:AXB_WDSEL + kc * 64 + 64]
                rhs = axC[:, kc * 2048 + ltb * 512: kc * 2048 + ltb * 512 + 512]
                if kc == C - 1:
                    pe_h = emi("tensor", lambda t, o=pb[3][0:64, 0:512], a=lhs, b=rhs:
                               t.matmul(o, a, b, start=False, stop=True), "pe")
                else:
                    emi("tensor", lambda t, o=pb[3][0:64, 0:512], a=lhs, b=rhs,
                        st=(kc == 0): t.matmul(o, a, b, start=st, stop=False))
            wait("scalar", "pe", pe_h)
            h_sc[ltb] = emi("scalar", lambda sc, ltb=ltb: sc.activation(
                axB[0:64, AXB_H + ltb * 512:AXB_H + ltb * 512 + 512],
                pb[3][0:64, 0:512], AF.Relu, bias=bdcol[:], scale=1.0), "sc")
        sc_h_done = cnt["sc"]
        dbg_out("hbuf", axB[0:65, AXB_H:AXB_H + 2048], [("sc", sc_h_done)])

        wait("tensor", "gb", gd_base)
        wait("tensor", "sc", sc_msel)
        wait("tensor", "ve", ve_br)
        wait("tensor", "ve", ve_hone)
        wait("tensor", "sc", sc_h_done)
        ffn_pe, o_sc, o_sd = {}, {}, {}
        for lt in range(LT):
            b0 = pb[0 + 2 * (lt % 2)]
            b1 = pb[1 + 2 * (lt % 2)]
            ls = lnst[:, (lt % 2) * 20:(lt % 2) * 20 + 20]
            nmr = ls[:, 16:17]
            rstd = ls[:, 15:16]
            if lt - 2 in o_sc:
                wait("tensor", "sc", o_sc[lt - 2])
            for oh in range(2):
                bank = b0 if oh == 0 else b1
                if lt > 0:
                    for kc in range(C):
                        lhs = axC[:, kc * 2048 + lt * 128:
                                  kc * 2048 + lt * 128 + 128]
                        rhs = wg[:, kc * 1024 + oh * 512:
                                 kc * 1024 + oh * 512 + 512]
                        emi("tensor", lambda t, o=bank[:, 0:512], a=lhs, b=rhs,
                            st=(kc == 0): t.matmul(o, a, b, start=st, stop=False))
                emi("tensor", lambda t, o=bank[:, 0:512], a=onesrow_s[:],
                    b=rowstg[0:1, oh * 512:oh * 512 + 512]:
                    t.matmul(o, a, b, start=False, stop=False))
                ffn_pe[(lt, oh)] = emi("tensor", lambda t, o=bank[:, 0:512],
                                       a=axB[0:65, AXB_H + lt * 128:
                                             AXB_H + lt * 128 + 128],
                                       b=axB[0:65, AXB_MSEL + oh * 512:
                                             AXB_MSEL + oh * 512 + 512]:
                                       t.matmul(o, a, b, start=False, stop=True), "pe")
            wait("vector", "pe", ffn_pe[(lt, 1)])
            emi("vector", lambda v, o=b0[:, 0:512], a=b0[:, 0:512],
                b=axA[:, lt * 1024:lt * 1024 + 512]: v.tensor_add(o, a, b))
            emi("vector", lambda v, o=b1[:, 0:512], a=b1[:, 0:512],
                b=axA[:, lt * 1024 + 512:lt * 1024 + 1024]: v.tensor_add(o, a, b),
                "ve")
            wait("vector", "ve", cnt["ve"])
            emi("vector", lambda v, o=ls[:, 0:6], a=b0[:, 0:512]: v.bn_stats(o, a))
            emi("vector", lambda v, o=ls[:, 6:12], a=b1[:, 0:512]: v.bn_stats(o, a),
                "ve")
            wait("vector", "ve", cnt["ve"])
            ve_bn = emi("vector", lambda v, o=ls[:, 12:14], a=ls[:, 0:12]:
                        v.bn_aggr(o, a), "ve")
            wait("scalar", "ve", ve_bn)
            emi("scalar", lambda sc, o=ls[:, 14:15], a=ls[:, 13:14]:
                sc.activation(o, a, AF.Ln, bias=epscol[:], scale=1.0), "sc")
            wait("scalar", "sc", cnt["sc"])
            sc_r = emi("scalar", lambda sc, o=rstd, a=ls[:, 14:15]:
                       sc.activation(o, a, AF.Exp, bias=0.0, scale=-0.5), "sc")
            wait("vector", "sc", sc_r)
            ve_nmr = emi("vector", lambda v, o=nmr, a=ls[:, 12:13], b=rstd:
                         v.scalar_tensor_tensor(o, a, -1.0, b, ALU.mult, ALU.mult),
                         "ve")
            wait("scalar", "ve", ve_nmr)
            if lt - 2 in o_sd:
                wait("scalar", "so0" if lt % 2 == 0 else "so1", o_sd[lt - 2])
            emi("scalar", lambda sc, o=iostg[:, (lt % 2) * 1024:(lt % 2) * 1024 + 512],
                a=b0[:, 0:512], b=nmr, r=rstd:
                sc.activation(o, a, AF.Identity, bias=b, scale=r))
            o_sc[lt] = emi("scalar", lambda sc,
                           o=iostg[:, (lt % 2) * 1024 + 512:(lt % 2) * 1024 + 1024],
                           a=b1[:, 0:512], b=nmr, r=rstd:
                           sc.activation(o, a, AF.Identity, bias=b, scale=r), "sc")
            wait("sync", "sc", o_sc[lt])
            o_sd[lt] = emi("sync", lambda s, lt=lt: s.dma_start(
                out_t[lt], iostg[:, (lt % 2) * 1024:(lt % 2) * 1024 + 1024]),
                "so0" if lt % 2 == 0 else "so1", 16)

        wait("sync", "so0", cnt["so0"])
        wait("sync", "so1", cnt["so1"])
        if cnt["sd"]:
            wait("sync", "sd", cnt["sd"])

        blk = ctx.enter_context(nc.Block())

        @blk.tensor
        def _(t):
            for f in opsq["tensor"]:
                f(t)

        @blk.scalar
        def _(s):
            for f in opsq["scalar"]:
                f(s)

        @blk.vector
        def _(v):
            for f in opsq["vector"]:
                f(v)

        @blk.sync
        def _(s):
            for f in opsq["sync"]:
                f(s)

        @blk.gpsimd
        def _(g):
            for f in opsq["gpsimd"]:
                f(g)

    return nc


def _chunkT(wT):
    m = wT.shape[1]
    return np.ascontiguousarray(
        wT.reshape(8, 128, m).transpose(1, 0, 2).reshape(128, 8 * m))


def prep_inputs(inp):
    f32 = np.float32
    Wq = np.asarray(inp["L0_attn_Wq"], f32)
    Wk = np.asarray(inp["L0_attn_Wk"], f32)
    Wv = np.asarray(inp["L0_attn_Wv"], f32)
    Wo = np.asarray(inp["L0_attn_Wo"], f32)
    Wrf = np.asarray(inp["L0_attn_Wrf"], f32)
    baseW = np.asarray(inp["L0_ffn_base_W"], f32)
    x = np.asarray(inp["L0_attn_x"], f32)

    shared = {}
    shared["wq"] = _chunkT(Wq.T)
    shared["wk"] = np.ascontiguousarray(
        Wk.T.reshape(8, 128, 8, 128).transpose(1, 2, 0, 3).reshape(128, 8192))
    shared["wv"] = _chunkT(Wv.T)
    shared["wo"] = _chunkT(Wo.T)
    shared["wbase"] = _chunkT(baseW.T)

    W0 = Wrf[0]
    blk = np.zeros((8, 128, 128), f32)
    for c in range(8):
        blk[c, 0:64, 0:64] = W0[2 * c]
        blk[c, 64:128, 64:128] = W0[2 * c + 1]
    shared["w0pair"] = np.ascontiguousarray(
        blk.transpose(1, 0, 2).reshape(128, 1024))

    downW = np.asarray(inp["L0_ffn_down_W"], f32)
    shared["downw"] = np.ascontiguousarray(
        downW.reshape(8, 8, 128, 64).transpose(2, 1, 0, 3).reshape(128, 4096))

    upW = np.asarray(inp["L0_ffn_up_W"], f32)
    upb = np.asarray(inp["L0_ffn_up_b"], f32)
    aug = np.concatenate([upW.transpose(0, 2, 1),
                          upb[:, :, None]], axis=2)
    shared["wuaug"] = np.ascontiguousarray(
        aug.reshape(8, 8, 128, 65).transpose(2, 0, 1, 3).reshape(128, 8 * 8 * 65))

    convW = np.asarray(inp["L0_ffn_conv_W"], f32)
    shared["wct"] = np.ascontiguousarray(
        convW.transpose(0, 2, 1).reshape(8, 8, 128, 1024)
        .transpose(2, 0, 1, 3).reshape(128, 8 * 8192))

    bq = np.asarray(inp["L0_attn_bq"], f32)
    bk = np.asarray(inp["L0_attn_bk"], f32)
    mu = np.asarray(inp["L0_ffn_disc_mu"], f32)
    sg = np.asarray(inp["L0_ffn_disc_sigma"], f32)
    bd = np.asarray(inp["L0_ffn_down_b"], f32)
    smalls = np.zeros((128, 152), f32)
    smalls[:, 0:8] = bq.reshape(8, 128).T
    smalls[:, 8:16] = bk.reshape(8, 128).T
    smalls[:, 16:80] = mu.T.reshape(8, 128, 8).transpose(1, 0, 2).reshape(128, 64)
    smalls[:, 80:144] = sg.T.reshape(8, 128, 8).transpose(1, 0, 2).reshape(128, 64)
    smalls[0:64, 144:152] = bd.T
    shared["smalls"] = smalls

    bv = np.asarray(inp["L0_attn_bv"], f32)
    bo = np.asarray(inp["L0_attn_bo"], f32)
    bb = np.asarray(inp["L0_ffn_base_b"], f32)
    shared["rows3"] = np.concatenate([bv, bo, bb])[None, :].astype(f32)
    shared["convb"] = np.asarray(inp["L0_ffn_conv_b"], f32)

    shared["identb"] = np.eye(128, dtype=f32)
    nm = np.zeros((128, 128), f32)
    nm[0:64, 0:64] = -0.5
    nm[64:128, 64:128] = -0.5
    shared["negmask"] = nm
    shared["onesrow_b"] = np.ones((1, 128), f32)
    shared["onescol_b"] = np.ones((128, 1), f32)
    shared["ones1f"] = np.ones((1, 128), f32)
    shared["onescolf"] = np.ones((128, 1), f32)
    b2 = np.zeros((128, 2), f32)
    b2[0:64, 0] = 1.0
    b2[64:128, 1] = 1.0
    shared["blockones2"] = b2
    i2 = np.zeros((2, 128), f32)
    i2[0, 0:64] = 1.0
    i2[1, 64:128] = 1.0
    shared["indic2"] = i2

    in_maps = []
    for b in range(NCORES):
        m = dict(shared)
        m["xT"] = np.ascontiguousarray(
            x[b].T.reshape(8, 128, 4, 512).transpose(1, 2, 0, 3).reshape(128, 16384))
        m["xtok"] = np.ascontiguousarray(x[b])
        in_maps.append(m)
    return in_maps


def kernel(**inputs):
    in_maps = prep_inputs(inputs)
    nc = build_program(DBG)
    res = run_bass_kernel_spmd(nc, in_maps, core_ids=list(range(NCORES)))
    return np.stack([np.asarray(r["out"], np.float32) for r in res.results], axis=0)


# version 35
# speedup vs baseline: 1.0038x; 1.0038x over previous
from contextlib import ExitStack
import numpy as np
import ml_dtypes
import concourse.bass as bass
import concourse.mybir as mybir
from concourse.bass_utils import run_bass_kernel_spmd

F32 = mybir.dt.float32
BF16 = mybir.dt.bfloat16
AF = mybir.ActivationFunctionType
ALU = mybir.AluOpType
AX = mybir.AxisListType

NCORES = 8
L = 2048
D = 1024
C = 8
LT = 16
E = 8
EPS_ATTN = 1e-6
EPS_LN = 1e-5

SCR_WU = 0
SCR_K2 = 1088
SCR_EK = 3136
SCR_MSTG = 7232

AXD_CTX = 0
AXD_CTXT = 2048
AXD_ZF = 2048
AXD_DW = 8192

AXB_MALL = 0
AXB_H = 8192
AXB_MSEL = 10240
AXB_WDSEL = 11264
AXB_ROW = 11776

DBG = ()
SCHED = []


def build_program(dbg=()):
    dbg = set(dbg)
    nc = bass.Bass("TRN2", target_bir_lowering=False)

    def din(name, shape, dt=F32):
        return nc.dram_tensor(name, shape, dt, kind="ExternalInput").ap()

    xT = din("xT", [128, C * L])
    xtok = din("xtok", [L, D])
    xtok_t = xtok.rearrange("(t p) d -> t p d", p=128)
    wq = din("wq", [128, C * D])
    wk = din("wk", [128, C * D])
    wv = din("wv", [128, C * D])
    wo = din("wo", [128, C * D])
    wbase = din("wbase", [128, C * D])
    w0pair = din("w0pair", [128, C * 128])
    downw = din("downw", [128, C * E * 64])
    wuaug = din("wuaug", [128, E * C * 65])
    wct = din("wct", [128, E * C * D])
    smalls = din("smalls", [128, 152])
    rows3 = din("rows3", [1, 3 * D])
    convb = din("convb", [E, D])
    identb = din("identb", [128, 128])
    negmask = din("negmask", [128, 128])
    onesrow_b = din("onesrow_b", [1, 128])
    onescol_b = din("onescol_b", [128, 1])
    ones1f = din("ones1f", [1, 128], F32)
    onescolf = din("onescolf", [128, 1], F32)
    blockones2 = din("blockones2", [128, 2], F32)
    indic2 = din("indic2", [2, 128], F32)

    out = nc.dram_tensor("out", [L, D], F32, kind="ExternalOutput").ap()
    out_t = out.rearrange("(t p) d -> t p d", p=128)
    mall_dram = nc.dram_tensor("mall_scratch", [65, E * D], BF16).ap()

    with ExitStack() as ctx:
        def sb(name, shape, dt=BF16):
            return ctx.enter_context(nc.sbuf_tensor(name, shape, dt))

        def pst(name):
            return ctx.enter_context(nc.psum_tensor(name, [128, 512], F32))

        axA = sb("axA", [128, 16384])
        axB = sb("axB", [128, 16384])
        axC = sb("axC", [128, 16384])
        axD = sb("axD", [128, 16384])
        scr = sb("scr", [128, 8256])
        wg = sb("wg", [128, 16384])
        iostg = sb("iostg", [128, 2048], F32)
        xstg = sb("xstg", [128, 2048])
        ndb = sb("ndb", [128, C * 130])
        smallsb = sb("smallsb", [128, 152], F32)
        rows3b = sb("rows3b", [1, 3 * D], BF16)
        identb_s = sb("identb_s", [128, 128])
        negmask_s = sb("negmask_s", [128, 128])
        w0pb = sb("w0pb", [128, C * 128])
        onesrow_s = sb("onesrow_s", [1, 128])
        onescol_s = sb("onescol_s", [128, 1])
        ones1f_s = sb("ones1f_s", [1, 128], F32)
        onescolf_s = sb("onescolf_s", [128, 1], F32)
        blk2_s = sb("blk2_s", [128, 2], F32)
        indic2_s = sb("indic2_s", [2, 128], F32)
        convb_s = sb("convb_s", [E, D])
        bvbc = sb("bvbc", [128, D])
        discw = sb("discw", [128, C * 24], F32)
        mtmp = sb("mtmp", [128, C * 8], F32)
        mstat = sb("mstat", [128, C * 24], F32)
        mcols = sb("mcols", [128, C * 2], F32)
        lossv = sb("lossv", [1, 16], F32)
        mask8 = sb("mask8", [8, 1])
        maskbc = sb("maskbc", [128, 8], F32)
        invz = sb("invz", [2, 1], F32)
        dcol = sb("dcol", [128, 1], F32)
        bdcol = sb("bdcol", [64, 1], F32)
        lnst = sb("lnst", [128, 40], F32)
        ddbuf = sb("ddbuf", [128, 24], F32)
        epscol = sb("epscol", [128, 1], F32)
        mxb = sb("mxb", [128, 24], F32)

        axDf = axD.bitcast(F32)
        axBf = axB.bitcast(F32)

        pb = [pst(f"pb{i}") for i in range(8)]
        pbb = [p.bitcast(BF16) for p in pb]

        SEM = {}
        for k in ("pe", "sc", "ve", "sd", "gp", "g0", "ga", "gb",
                  "gc", "gw", "g1", "gx", "xq1", "xq2", "xq3", "g2", "s0",
                  "sx0", "sx1", "sx2", "sx3", "so0", "so1", "sm", "sm2", "sm3"):
            SEM[k] = ctx.enter_context(nc.semaphore("s_" + k))
        cnt = dict(pe=0, sc=0, ve=0, sd=0, gp=0, g0=0, ga=0, gb=0,
                   gc=0, gw=0, g1=0, gx=0, xq1=0, xq2=0, s0=0, sx0=0, sx1=0,
                   so0=0, so1=0, sm=0, sm2=0, sm3=0, xq3=0, g2=0, sx2=0, sx3=0)
        opsq = {k: [] for k in ("tensor", "scalar", "vector", "sync", "gpsimd")}

        def em(eng, fn):
            opsq[eng].append(fn)

        SCHED.clear()

        def wait(eng, key, v):
            if v and v > 0:
                SCHED.append(("wait", eng, key, v))
                em(eng, lambda e, s=SEM[key], v=v: e.wait_ge(s, v))

        def emi(eng, build, inckey=None, n=1):
            if inckey is None:
                SCHED.append(("op", eng, None, 0))
                em(eng, build)
                return None
            cnt[inckey] += n
            SCHED.append(("op", eng, inckey, n))
            em(eng, lambda e, b=build, s=SEM[inckey], n=n: b(e).then_inc(s, n))
            return cnt[inckey]



        def dbg_out(name, src_ap, gates):
            if name not in dbg:
                return
            ap = nc.dram_tensor("dbg_" + name, list(src_ap.shape), src_ap.dtype,
                                kind="ExternalOutput").ap()
            for k, v in gates:
                wait("sync", k, v)
            emi("sync", lambda s, a=ap, b=src_ap: s.dma_start(a, b), "sd", 16)

        for dst, src in ((ones1f_s, ones1f), (onescolf_s, onescolf),
                         (blk2_s, blockones2), (indic2_s, indic2),
                         (smallsb, smalls)):
            emi("sync", lambda s, d=dst, x=src: s.dma_start(d[:], x), "s0", 16)
        sd_consts = cnt["s0"]
        for dst, src in ((identb_s, identb), (negmask_s, negmask),
                         (onesrow_s, onesrow_b), (onescol_s, onescol_b)):
            emi("gpsimd", lambda g, d=dst, x=src: g.dma_start(d[:], x), "g0", 16)

        emi("gpsimd", lambda g: g.dma_start(wg[:, 0:4096], wk[:, 0:4096]), "g1", 16)
        emi("gpsimd", lambda g: g.dma_start(axA[:, 0:4096], xT[:, 0:4096]), "g1", 16)
        wait("gpsimd", "g1", cnt["g1"])
        emi("gpsimd", lambda g: g.dma_start(wg[:, 4096:8192], wk[:, 4096:8192]),
            "gx", 16)
        wait("gpsimd", "gx", cnt["gx"])
        emi("gpsimd", lambda g: g.dma_start(axA[:, 4096:8192], xT[:, 4096:8192]),
            "xq1", 16)
        wait("gpsimd", "xq1", cnt["xq1"])
        emi("gpsimd", lambda g: g.dma_start(axA[:, 8192:12288], xT[:, 8192:12288]),
            "xq2", 16)
        emi("gpsimd", lambda g: g.dma_start(axA[:, 12288:16384], xT[:, 12288:16384]),
            "xq3", 16)
        emi("gpsimd", lambda g: g.dma_start(wg[:, 8192:16384], wv), "gb", 16)
        emi("gpsimd", lambda g: g.dma_start(w0pb[:], w0pair), "g2", 16)
        emi("gpsimd", lambda g: g.dma_start(rows3b[:], rows3), "g2", 16)
        emi("gpsimd", lambda g: g.dma_start(convb_s[:], convb), "g2", 16)
        g0_all = cnt["g0"]
        g1_all = cnt["g1"]
        gx_all = cnt["gx"]
        g2_all = cnt["g2"]
        xq1_all = cnt["xq1"]
        xq2_all = cnt["xq2"]
        xq3_all = cnt["xq3"]
        gd_wv = cnt["gb"]

        from concourse.hw_specs import get_activation_tables
        nlx_idx = list(get_activation_tables(nc.m.arch).keys()).index(
            "natural_log_exp_and_others")

        def _preload(sc):
            return sc.add_instruction(mybir.InstLoadActFuncSet(
                name=nc.get_next_instruction_name(),
                act_func_set_id=nlx_idx, ins=[], outs=[]))
        em("scalar", _preload)

        ve_eps = emi("vector", lambda v: v.memset(epscol[:], EPS_LN), "ve")

        wait("vector", "s0", sd_consts)
        for step in range(6):
            if step:
                wait("vector", "ve", cnt["ve"])
            for c in range(C):
                mu = smallsb[:, 16 + c * 8:24 + c * 8]
                sg = smallsb[:, 80 + c * 8:88 + c * 8]
                ivd = discw[:, c * 24:c * 24 + 8]
                n2 = discw[:, c * 24 + 8:c * 24 + 16]
                m2w = discw[:, c * 24 + 16:c * 24 + 24]
                mi = mtmp[:, c * 8:c * 8 + 8]
                ik = "ve" if c == C - 1 else None
                if step == 0:
                    emi("vector", lambda v, o=ivd, a=sg: v.tensor_mul(o, a, a), ik)
                elif step == 1:
                    emi("vector", lambda v, o=ivd: v.reciprocal(o, o), ik)
                elif step == 2:
                    emi("vector", lambda v, o=mi, a=mu, b=ivd: v.tensor_mul(o, a, b),
                        ik)
                elif step == 3:
                    emi("vector", lambda v, o=m2w, a=mu, b=mi: v.tensor_mul(o, a, b),
                        ik)
                elif step == 4:
                    emi("vector", lambda v, o=n2, a=mi: v.tensor_scalar(
                        o, a, -2.0 / 1024.0, None, ALU.mult), ik)
                else:
                    emi("vector", lambda v, o=ivd, a=ivd: v.tensor_scalar(
                        o, a, 1.0 / 1024.0, None, ALU.mult), ik)
        wait("vector", "ve", cnt["ve"])
        for c in range(C):
            m2w = discw[:, c * 24 + 16:c * 24 + 24]
            emi("vector", lambda v, o=m2w, a=m2w: v.tensor_scalar(
                o, a, 1.0 / 1024.0, None, ALU.mult), "ve" if c == C - 1 else None)
        ve_disc = cnt["ve"]

        ve_ndbinit = emi("vector", lambda v: v.memset(ndb[:], 0.0), "ve")

        wait("tensor", "g1", g1_all)
        wait("tensor", "s0", sd_consts)
        wait("scalar", "s0", sd_consts)
        kmark = {}
        pe_last = 0
        for i in range(32):
            ltb, mc = divmod(i, 8)
            bank = pb[i % 2]
            if i - 2 in kmark:
                wait("tensor", "sc", kmark[i - 2])
            if i == 4:
                wait("tensor", "gx", gx_all)
            if i == 8:
                wait("tensor", "xq1", xq1_all)
            if i == 16:
                wait("tensor", "xq2", xq2_all)
            if i == 24:
                wait("tensor", "xq3", xq3_all)
            for kc in range(C):
                lhs = wg[:, mc * 1024 + kc * 128: mc * 1024 + kc * 128 + 128]
                rhs = axA[:, ltb * 4096 + kc * 512: ltb * 4096 + kc * 512 + 512]
                if kc == C - 1:
                    pe_last = emi("tensor", lambda t, o=bank[:, 0:512], a=lhs, b=rhs:
                                  t.matmul(o, a, b, start=False, stop=True), "pe")
                else:
                    emi("tensor", lambda t, o=bank[:, 0:512], a=lhs, b=rhs,
                        st=(kc == 0): t.matmul(o, a, b, start=st, stop=False))
            wait("scalar", "pe", pe_last)
            dst = axD[:, mc * 2048 + ltb * 512: mc * 2048 + ltb * 512 + 512]
            bap = smallsb[:, 8 + mc:9 + mc]
            kmark[i] = emi("scalar", lambda sc, o=dst, i_=bank[:, 0:512], b=bap:
                           sc.activation(o, i_, AF.Identity, bias=b, scale=1.0), "sc")
        pe_k_done = pe_last
        dbg_out("kTb", axD[:], [("sc", cnt["sc"])])

        wait("tensor", "g0", g0_all)
        wait("tensor", "g2", g2_all)
        emi("tensor", lambda t: t.matmul(pb[7][:, 0:512], onesrow_s[:],
                                         rows3b[0:1, 0:512], start=True, stop=True))
        pe_bv = emi("tensor", lambda t: t.matmul(pb[6][:, 0:512], onesrow_s[:],
                                                 rows3b[0:1, 512:1024],
                                                 start=True, stop=True), "pe")
        wait("vector", "pe", pe_bv)
        emi("vector", lambda v: v.tensor_copy(bvbc[:, 0:512], pb[7][:, 0:512]))
        ve_bvbc = emi("vector", lambda v: v.tensor_copy(bvbc[:, 512:1024],
                                                        pb[6][:, 0:512]), "ve")

        wait("gpsimd", "pe", pe_k_done)
        gd_wq = emi("gpsimd", lambda g: g.dma_start(wg[:, 0:8192], wq), "ga", 16)

        wait("tensor", "gb", gd_wv)
        vmark = {}
        for i in range(32):
            lt, oh = divmod(i, 2)
            bank = pb[2 + i % 2]
            if i - 2 in vmark:
                wait("tensor", "ve", vmark[i - 2])
            for kc in range(C):
                lhs = axA[:, (lt // 4) * 4096 + kc * 512 + (lt % 4) * 128:
                          (lt // 4) * 4096 + kc * 512 + (lt % 4) * 128 + 128]
                rhs = wg[:, 8192 + kc * 1024 + oh * 512:
                         8192 + kc * 1024 + oh * 512 + 512]
                if kc == C - 1:
                    pe_last = emi("tensor", lambda t, o=bank[:, 0:512], a=lhs, b=rhs:
                                  t.matmul(o, a, b, start=False, stop=True), "pe")
                else:
                    emi("tensor", lambda t, o=bank[:, 0:512], a=lhs, b=rhs,
                        st=(kc == 0): t.matmul(o, a, b, start=st, stop=False))
            wait("vector", "pe", pe_last)
            if i == 0:
                wait("vector", "ve", ve_bvbc)
            dst = axC[:, lt * 1024 + oh * 512: lt * 1024 + oh * 512 + 512]
            vmark[i] = emi("vector", lambda v, o=dst, a=bank[:, 0:512],
                           b=bvbc[:, oh * 512:oh * 512 + 512]:
                           v.tensor_add(o, a, b), "ve")
        pe_v_done = pe_last
        ve_v_done = cnt["ve"]
        dbg_out("vb", axC[:], [("ve", ve_v_done)])

        wait("gpsimd", "pe", pe_v_done)
        gd_wo = emi("gpsimd", lambda g: g.dma_start(wg[:, 8192:16384], wo), "gb", 16)

        wait("tensor", "ga", gd_wq)
        wait("tensor", "ve", ve_v_done)
        wait("tensor", "sc", kmark[31])
        qmark, sqsc, trve = {}, {}, {}

        def p3_q(i):
            c, ltb = divmod(i, 4)
            qbank = pb[i % 2]
            qslot = scr[:, (i % 2) * 512:(i % 2) * 512 + 512]
            if i - 2 in qmark:
                wait("tensor", "ve", qmark[i - 2])
            for kc in range(C):
                lhs = wg[:, kc * 1024 + c * 128: kc * 1024 + c * 128 + 128]
                rhs = axA[:, ltb * 4096 + kc * 512: ltb * 4096 + kc * 512 + 512]
                if kc == C - 1:
                    pe_last = emi("tensor", lambda t, o=qbank[:, 0:512], a=lhs, b=rhs:
                                  t.matmul(o, a, b, start=False, stop=True), "pe")
                else:
                    emi("tensor", lambda t, o=qbank[:, 0:512], a=lhs, b=rhs,
                        st=(kc == 0): t.matmul(o, a, b, start=st, stop=False))
            wait("vector", "pe", pe_last)
            bap = smallsb[:, c:c + 1]
            qmark[i] = emi("vector", lambda v, o=qslot, a=qbank[:, 0:512], b=bap:
                           v.tensor_scalar(o, a, b, None, ALU.add), "ve")

        def p3_sq(i):
            c, ltb = divmod(i, 4)
            sbank = pb[2 + i % 2]
            qslot = scr[:, (i % 2) * 512:(i % 2) * 512 + 512]
            estage = scr[:, SCR_EK + (i % 3) * 512: SCR_EK + (i % 3) * 512 + 512]
            mxs = mxb[:, (i % 3) * 8:(i % 3) * 8 + 8]
            if i - 2 in sqsc:
                wait("tensor", "sc", sqsc[i - 2])
            wait("tensor", "ve", qmark[i])
            for j in range(4):
                oc = sbank[:, j * 128:j * 128 + 128]
                lhs = qslot[:, j * 128:j * 128 + 128]
                if j == 3:
                    pe_sq = emi("tensor", lambda t, o=oc, a=lhs,
                                b=w0pb[:, c * 128:c * 128 + 128]:
                                t.matmul(o, a, b, start=True, stop=True), "pe")
                else:
                    emi("tensor", lambda t, o=oc, a=lhs,
                        b=w0pb[:, c * 128:c * 128 + 128]:
                        t.matmul(o, a, b, start=True, stop=True))
            wait("vector", "pe", pe_sq)
            s3 = sbank[:].rearrange("p (j h f) -> p j h f", j=4, h=2)
            emi("vector", lambda v, o=mxs[:, 0:4], a=s3[:, :, 0, :]:
                v.tensor_reduce(o, a, AX.X, ALU.max, negate=True))
            ve_mx = emi("vector", lambda v, o=mxs[:, 4:8], a=s3[:, :, 1, :]:
                        v.tensor_reduce(o, a, AX.X, ALU.max, negate=True), "ve")
            wait("scalar", "ve", ve_mx)
            for j in range(4):
                emi("scalar", lambda sc, o=estage[:, j * 128:j * 128 + 64],
                    i_=sbank[:, j * 128:j * 128 + 64], b=mxs[:, j:j + 1]:
                    sc.activation(o, i_, AF.Exp, bias=b, scale=1.0))
                sqsc[i] = emi("scalar", lambda sc,
                              o=estage[:, j * 128 + 64:j * 128 + 128],
                              i_=sbank[:, j * 128 + 64:j * 128 + 128],
                              b=mxs[:, 4 + j:5 + j]:
                              sc.activation(o, i_, AF.Exp, bias=b, scale=1.0), "sc")

        def p3_tr(i):
            c, ltb = divmod(i, 4)
            tbank = pbb[6 + i % 2]
            estage = scr[:, SCR_EK + (i % 3) * 512: SCR_EK + (i % 3) * 512 + 512]
            wait("tensor", "sc", sqsc[i])
            if i - 2 in trve:
                wait("tensor", "ve", trve[i - 2])
            for j in range(4):
                if j == 3:
                    pe_tr = emi("tensor", lambda t,
                                o=tbank[:, j * 128:j * 128 + 128],
                                a=estage[:, j * 128:j * 128 + 128]:
                                t.transpose(o, a, identb_s[:]), "pe")
                else:
                    emi("tensor", lambda t, o=tbank[:, j * 128:j * 128 + 128],
                        a=estage[:, j * 128:j * 128 + 128]:
                        t.transpose(o, a, identb_s[:]))
            wait("vector", "pe", pe_tr)
            trve[i] = emi("vector", lambda v,
                          o=axB[:, c * 2048 + ltb * 512:c * 2048 + ltb * 512 + 512],
                          a=tbank[:, 0:512]: v.tensor_copy(o, a), "ve")

        for s in range(34):
            if s < 32:
                p3_q(s)
            if 1 <= s <= 32:
                p3_sq(s - 1)
            if s >= 2:
                p3_tr(s - 2)
        pe_q_done = cnt["pe"]
        sc_eq_done = cnt["sc"]
        ve_eq_done = cnt["ve"]
        dbg_out("EqT", axB[:], [("ve", ve_eq_done)])

        wct_gd, wu_gd, me_pe, me_sc, me_sd = {}, {}, {}, {}, {}

        def sched_wct_dma(e, gate_pe):
            if gate_pe:
                wait("gpsimd", "pe", gate_pe)
            wct_gd[e] = emi("gpsimd", lambda g, e=e: g.dma_start(
                wg[:, 0:8192], wct[:, e * 8192:(e + 1) * 8192]), "gc", 16)
            wu_gd[e] = emi("gpsimd", lambda g, e=e: g.dma_start(
                scr[:, SCR_WU + (e % 2) * 520: SCR_WU + (e % 2) * 520 + 520],
                wuaug[:, e * 520:(e + 1) * 520]), "gw", 16)

        def emit_me(e):
            wait("tensor", "gc", wct_gd[e])
            wait("tensor", "gw", wu_gd[e])
            for half in range(2):
                if half == 1 or e >= 1:
                    wait("tensor", "sc", me_sc.get(e, me_sc.get(e - 1)))
                for kc in range(C):
                    lhs = scr[:, SCR_WU + (e % 2) * 520 + kc * 65:
                              SCR_WU + (e % 2) * 520 + kc * 65 + 65]
                    rhs = wg[:, kc * 1024 + half * 512:
                             kc * 1024 + half * 512 + 512]
                    if kc == C - 1:
                        me_pe[e] = emi("tensor", lambda t, o=pb[7][0:65, 0:512],
                                       a=lhs, b=rhs:
                                       t.matmul(o, a, b, start=False, stop=True), "pe")
                    else:
                        emi("tensor", lambda t, o=pb[7][0:65, 0:512], a=lhs, b=rhs,
                            st=(kc == 0): t.matmul(o, a, b, start=st, stop=False))
                wait("scalar", "pe", me_pe[e])
                if half == 0 and e >= 1:
                    wait("scalar", "sm", me_sd[e - 1])
                me_sc[e] = emi("scalar", lambda sc, half=half: sc.copy(
                    scr[0:65, SCR_MSTG + half * 512:SCR_MSTG + half * 512 + 512],
                    pb[7][0:65, 0:512]), "sc")
            wait("sync", "sc", me_sc[e])
            me_sd[e] = emi("sync", lambda s, e=e: s.dma_start(
                mall_dram[:, e * D:(e + 1) * D],
                scr[0:65, SCR_MSTG:SCR_MSTG + 1024]), "sm", 16)
            if e + 1 < E:
                sched_wct_dma(e + 1, me_pe[e])

        def emit_me_next():
            done = set(me_pe)
            for e in range(E):
                if e in wct_gd and e not in done:
                    emit_me(e)
                    return

        sched_wct_dma(0, pe_q_done)

        wait("tensor", "ve", ve_ndbinit)
        wait("tensor", "ve", ve_eq_done)
        wait("tensor", "sc", sc_eq_done)
        ek_sc, nd_ve, ekpe, npe, k2ve = {}, {}, {}, {}, {}

        def p4_dcol(c):
            nbank = pb[2 + c % 2]
            wait("vector", "pe", npe[c])
            zv["d", c] = emi("vector", lambda v, a=nbank[:, 128:129]:
                             v.tensor_copy(dcol[:], a), "ve")

        def p4_zpair(c):
            wait("tensor", "ve", zv["d", c])
            zv["zp", c] = emi("tensor", lambda t: t.matmul(
                pb[5][0:2, 0:1], blk2_s[:], dcol[:], start=True, stop=True), "pe")

        def p4_recip(c):
            wait("vector", "pe", zv["zp", c])
            zv["iz", c] = emi("vector", lambda v: v.reciprocal(
                invz[:], pb[5][0:2, 0:1]), "ve")

        def p4_bc(c):
            wait("tensor", "ve", zv["iz", c])
            zv["bc", c] = emi("tensor", lambda t: t.matmul(
                pb[6][:, 0:1], indic2_s[:], invz[:], start=True, stop=True), "pe")

        def p4_ndb_evac(c):
            nbank = pb[2 + c % 2]
            wait("vector", "pe", zv["bc", c])
            emi("vector", lambda v, c=c, nb=nbank: v.tensor_scalar(
                ndb[0:64, c * 130:c * 130 + 64], nb[0:64, 0:64],
                pb[6][0:64, 0:1], None, ALU.mult))
            emi("vector", lambda v, c=c, nb=nbank: v.tensor_scalar(
                ndb[64:128, c * 130 + 65:c * 130 + 129], nb[64:128, 64:128],
                pb[6][64:128, 0:1], None, ALU.mult))
            emi("vector", lambda v, c=c, nb=nbank: v.tensor_scalar(
                ndb[0:64, c * 130 + 64:c * 130 + 65], nb[0:64, 128:129],
                pb[6][0:64, 0:1], EPS_ATTN, ALU.mult, ALU.add))
            nd_ve[c] = emi("vector", lambda v, c=c, nb=nbank: v.tensor_scalar(
                ndb[64:128, c * 130 + 129:c * 130 + 130], nb[64:128, 128:129],
                pb[6][64:128, 0:1], EPS_ATTN, ALU.mult, ALU.add), "ve")

        zv = {}
        k2lb = {}
        for c in range(C):
            if c - 1 in ekpe:
                wait("vector", "pe", ekpe[c - 1])
            for lb in range(4):
                src2 = axD[:, c * 2048 + lb * 512: c * 2048 + lb * 512 + 512]
                dstk = scr[:, SCR_K2 + lb * 512: SCR_K2 + lb * 512 + 512]
                k2lb[(c, lb)] = emi("vector", lambda v, o=dstk, a=src2:
                                    v.tensor_mul(o, a, a), "ve")
            if c >= 1:
                p4_dcol(c - 1)

            for lb in range(4):
                bank = pb[lb % 2]
                prev = ek_sc.get((c, lb - 2)) or ek_sc.get((c - 1, lb + 2))
                if prev:
                    wait("tensor", "sc", prev)
                wait("tensor", "ve", k2lb[(c, lb)])
                pe_ek = 0
                for j in range(4):
                    lt = lb * 4 + j
                    lhs1 = axD[:, c * 2048 + lt * 128: c * 2048 + lt * 128 + 128]
                    lhs2 = scr[:, SCR_K2 + lt * 128: SCR_K2 + lt * 128 + 128]
                    oc = bank[:, j * 128:j * 128 + 128]
                    emi("tensor", lambda t, o=oc, a=lhs1,
                        b=w0pb[:, c * 128:c * 128 + 128]:
                        t.matmul(o, a, b, start=True, stop=False))
                    if j == 3:
                        pe_ek = emi("tensor", lambda t, o=oc, a=lhs2, b=negmask_s[:]:
                                    t.matmul(o, a, b, start=False, stop=True), "pe")
                    else:
                        emi("tensor", lambda t, o=oc, a=lhs2, b=negmask_s[:]:
                            t.matmul(o, a, b, start=False, stop=True))
                ekpe[c] = pe_ek
                if lb == 0 and c >= 1:
                    p4_zpair(c - 1)
                    p4_recip(c - 1)
                wait("scalar", "pe", pe_ek)
                edst = scr[:, SCR_EK + (c % 2) * 2048 + lb * 512:
                           SCR_EK + (c % 2) * 2048 + lb * 512 + 512]
                ek_sc[(c, lb)] = emi("scalar", lambda sc, o=edst, i_=bank[:, 0:512]:
                                     sc.activation(o, i_, AF.Exp, bias=0.0,
                                                   scale=1.0), "sc")

            nbank = pb[2 + c % 2]
            if c - 2 in nd_ve:
                wait("tensor", "ve", nd_ve[c - 2])
            first = True
            for lb in range(4):
                wait("tensor", "sc", ek_sc[(c, lb)])
                for j in range(4):
                    lt = lb * 4 + j
                    lhs = scr[:, SCR_EK + (c % 2) * 2048 + lt * 128:
                              SCR_EK + (c % 2) * 2048 + lt * 128 + 128]
                    rhs = axC[:, lt * 1024 + c * 128: lt * 1024 + c * 128 + 128]
                    emi("tensor", lambda t, o=nbank[:, 0:128], a=lhs, b=rhs,
                        st=first: t.matmul(o, a, b, start=st, stop=False))
                    if lb == 3 and j == 3:
                        npe[c] = emi("tensor", lambda t, o=nbank[:, 128:129], a=lhs,
                                     b=onescol_s[:]:
                                     t.matmul(o, a, b, start=False, stop=True), "pe")
                    else:
                        emi("tensor", lambda t, o=nbank[:, 128:129], a=lhs,
                            b=onescol_s[:]:
                            t.matmul(o, a, b, start=False, stop=False))
                    first = False
            if c >= 1:
                p4_bc(c - 1)
                p4_ndb_evac(c - 1)
            if c in (3, 7):
                emit_me_next()
        p4_dcol(C - 1)
        p4_zpair(C - 1)

        wait("gpsimd", "pe", ekpe[C - 1])
        gd_dw = emi("gpsimd", lambda g: g.dma_start(
            axD[:, AXD_DW:AXD_DW + 4096], downw), "ga", 16)

        wait("tensor", "gb", gd_wo)

        def xslot(lt):
            if lt % 4 < 2:
                return xstg[:, (lt % 4) * 1024:(lt % 4) * 1024 + 1024]
            return scr[:, SCR_K2 + (lt % 4 - 2) * 1024:
                       SCR_K2 + (lt % 4 - 2) * 1024 + 1024]

        XSEM = ("sx0", "sx1", "sx2", "sx3")
        xsd = {}
        for lt0 in range(4):
            xsd[lt0] = emi("gpsimd", lambda g, lt=lt0: g.dma_start(
                xslot(lt), xtok_t[lt]), XSEM[lt0 % 4], 16)

        num_sc, ctxt_ve, x2t_ve, x2_sc, wo_pe, trb_pe = {}, {}, {}, {}, {}, {}
        cw_sc = {}
        p5mk = {}
        mpend, mq_mark = [], {}
        num_grp = {}

        AGRP = ((0, 3), (3, 6), (6, 8))

        def p5_A(lt, gset=None):
            for g, (c0, c1) in enumerate(AGRP):
                if gset is not None and g not in gset:
                    continue
                G = lt * 3 + g
                bank = pb[G % 3]
                if lt == 0:
                    wait("tensor", "ve", nd_ve[c1 - 1])
                if G - 3 in num_grp:
                    wait("tensor", "sc", num_grp[G - 3])
                pe_num = 0
                for c in range(c0, c1):
                    lhs = axB[:, c * 2048 + lt * 128: c * 2048 + lt * 128 + 128]
                    pe_num = emi("tensor", lambda t,
                                 o=bank[:, (c - c0) * 130:(c - c0) * 130 + 130],
                                 a=lhs, b=ndb[:, c * 130:(c + 1) * 130]:
                                 t.matmul(o, a, b, start=True, stop=True), "pe")
                if G == LT * 3 - 1:
                    p5mk["numdone"] = pe_num
                wait("vector", "pe", pe_num)
                if G - 4 in num_grp:
                    wait("vector", "sc", num_grp[G - 4])
                n = c1 - c0
                b3 = bank[:, 0:n * 130].rearrange("p (c g x) -> p c g x", c=n, g=2)
                rbase = (G % 4) * 6
                rs3 = ddbuf[:, rbase:rbase + n * 2].rearrange(
                    "p (c g) -> p c g", c=n)
                ve_r = emi("vector", lambda v, o=rs3, a=b3[:, :, :, 64]:
                           v.reciprocal(o, a), "ve")
                wait("scalar", "ve", ve_r)
                if lt - 2 in trb_pe and g == 0:
                    wait("scalar", "pe", trb_pe[lt - 2])
                for c in range(c0, c1):
                    i = lt * C + c
                    off = (c - c0) * 130
                    cbase = AXD_CTX + (lt % 2) * 1024 + c * 128
                    emi("scalar", lambda sc, o=axD[:, cbase:cbase + 64],
                        i_=bank[:, off:off + 64],
                        r=ddbuf[:, rbase + (c - c0) * 2:rbase + (c - c0) * 2 + 1]:
                        sc.activation(o, i_, AF.Identity, bias=0.0, scale=r))
                    num_sc[i] = emi("scalar", lambda sc,
                                    o=axD[:, cbase + 64:cbase + 128],
                                    i_=bank[:, off + 65:off + 129],
                                    r=ddbuf[:, rbase + (c - c0) * 2 + 1:
                                            rbase + (c - c0) * 2 + 2]:
                                    sc.activation(o, i_, AF.Identity, bias=0.0,
                                                  scale=r), "sc")
                num_grp[G] = num_sc[lt * C + c1 - 1]

        def p5_B_mm(lt):
            if lt - 1 in ctxt_ve:
                wait("tensor", "ve", ctxt_ve[lt - 1])
            for c in range(C):
                wait("tensor", "sc", num_sc[lt * C + c])
                src = axD[:, AXD_CTX + (lt % 2) * 1024 + c * 128:
                          AXD_CTX + (lt % 2) * 1024 + c * 128 + 128]
                if c == C - 1:
                    trb_pe[lt] = emi("tensor",
                                     lambda t, o=pbb[3][:, c * 128:c * 128 + 128],
                                     a=src: t.transpose(o, a, identb_s[:]), "pe")
                else:
                    emi("tensor", lambda t, o=pbb[3][:, c * 128:c * 128 + 128],
                        a=src: t.transpose(o, a, identb_s[:]))

        def p5_B_evac(lt):
            wait("vector", "pe", trb_pe[lt])
            if lt - 2 in wo_pe:
                wait("vector", "pe", wo_pe[lt - 2])
            ctxt_ve[lt] = emi("vector", lambda v,
                              o=axD[:, AXD_CTXT + (lt % 2) * 1024:
                                    AXD_CTXT + (lt % 2) * 1024 + 1024],
                              a=pbb[3][:, 0:1024]: v.tensor_copy(o, a), "ve")

        def p5_C_mm(lt):
            wait("tensor", "ve", ctxt_ve[lt])
            mark = 0
            if lt - 1 in cw_sc:
                wait("tensor", "ve", cw_sc[lt - 1])
            for oh in range(2):
                bank = pb[4 + oh]
                for kc in range(C):
                    lhs = axD[:, AXD_CTXT + (lt % 2) * 1024 + kc * 128:
                              AXD_CTXT + (lt % 2) * 1024 + kc * 128 + 128]
                    rhs = wg[:, 8192 + kc * 1024 + oh * 512:
                             8192 + kc * 1024 + oh * 512 + 512]
                    emi("tensor", lambda t, o=bank[:, 0:512], a=lhs, b=rhs,
                        st=(kc == 0): t.matmul(o, a, b, start=st, stop=False))
                mark = emi("tensor", lambda t, o=bank[:, 0:512], a=onesrow_s[:],
                           b=rows3b[0:1, 1024 + oh * 512:1024 + oh * 512 + 512]:
                           t.matmul(o, a, b, start=False, stop=True), "pe")
            wo_pe[lt] = mark

        def p5_CW(lt):
            stg = iostg[:, (lt % 2) * 1024:(lt % 2) * 1024 + 1024]
            xsl = xslot(lt)
            wait("vector", "pe", wo_pe[lt])
            wait("vector", XSEM[lt % 4], xsd[lt])
            if lt - 2 in x2_sc:
                wait("vector", "sc", x2_sc[lt - 2])
            emi("vector", lambda v, o=stg[:, 0:512], a=pb[4][:, 0:512],
                b=xsl[:, 0:512]: v.tensor_add(o, a, b))
            cw_sc[lt] = emi("vector", lambda v, o=stg[:, 512:1024],
                            a=pb[5][:, 0:512], b=xsl[:, 512:1024]:
                            v.tensor_add(o, a, b), "ve")
            if lt + 4 < LT:
                wait("gpsimd", "ve", cw_sc[lt])
                xsd[lt + 4] = emi("gpsimd", lambda g, lt2=lt + 4: g.dma_start(
                    xslot(lt2), xtok_t[lt2]), XSEM[lt % 4], 16)

        def p5_C_ln(lt):
            ls = lnst[:, (lt % 2) * 20:(lt % 2) * 20 + 20]
            nmr = ls[:, 16:17]
            rstd = ls[:, 15:16]
            stg = iostg[:, (lt % 2) * 1024:(lt % 2) * 1024 + 1024]
            wait("vector", "ve", cw_sc[lt])
            emi("vector", lambda v, o=ls[:, 0:6], a=stg[:, 0:512]:
                v.bn_stats(o, a))
            emi("vector", lambda v, o=ls[:, 6:12], a=stg[:, 512:1024]:
                v.bn_stats(o, a), "ve")
            wait("vector", "ve", cnt["ve"])
            ve_bn = emi("vector", lambda v, o=ls[:, 12:14], a=ls[:, 0:12]:
                        v.bn_aggr(o, a), "ve")
            wait("scalar", "ve", ve_bn)
            emi("scalar", lambda sc, o=ls[:, 14:15], a=ls[:, 13:14]:
                sc.activation(o, a, AF.Ln, bias=epscol[:], scale=1.0), "sc")
            wait("scalar", "sc", cnt["sc"])
            sc_r = emi("scalar", lambda sc, o=rstd, a=ls[:, 14:15]:
                       sc.activation(o, a, AF.Exp, bias=0.0, scale=-0.5), "sc")
            wait("vector", "sc", sc_r)
            ve_nmr = emi("vector", lambda v, o=nmr, a=ls[:, 12:13], b=rstd:
                         v.scalar_tensor_tensor(o, a, -1.0, b, ALU.mult, ALU.mult),
                         "ve")
            wait("scalar", "ve", ve_nmr)
            emi("scalar", lambda sc, o=axA[:, lt * 1024:lt * 1024 + 512],
                a=stg[:, 0:512], b=nmr, r=rstd:
                sc.activation(o, a, AF.Identity, bias=b, scale=r), "sc")
            x2_sc[lt] = emi("scalar", lambda sc,
                            o=axA[:, lt * 1024 + 512:lt * 1024 + 1024],
                            a=stg[:, 512:1024], b=nmr, r=rstd:
                            sc.activation(o, a, AF.Identity, bias=b, scale=r), "sc")

        def p5_D(lt):
            wait("tensor", "sc", x2_sc[lt])
            if lt - 1 in x2t_ve:
                wait("tensor", "ve", x2t_ve[lt - 1])
            for c in range(C):
                src = axA[:, lt * 1024 + c * 128: lt * 1024 + c * 128 + 128]
                if c == C - 1:
                    pe_x2 = emi("tensor",
                                lambda t, o=pbb[6][:, c * 128:c * 128 + 128],
                                a=src: t.transpose(o, a, identb_s[:]), "pe")
                else:
                    emi("tensor", lambda t, o=pbb[6][:, c * 128:c * 128 + 128],
                        a=src: t.transpose(o, a, identb_s[:]))
            wait("vector", "pe", pe_x2)
            dst3 = axC[:].rearrange("p (c l) -> p c l", c=C)[:, :,
                                                            lt * 128:lt * 128 + 128]
            src3 = pbb[6][:].rearrange("p (c l) -> p c l", c=C)
            x2t_ve[lt] = emi("vector", lambda v, o=dst3, a=src3:
                             v.tensor_copy(o, a), "ve")
            if lt % 4 == 3:
                q4 = lt // 4
                mq_mark[q4] = x2t_ve[lt]
                mpend.extend((c, q4) for c in range(C))
            for _ in range(2):
                if not mpend:
                    break
                c, q = mpend.pop(0)
                wait("vector", "ve", mq_mark[q])
                emi("vector", lambda v, c=c, q=q: v.bn_stats(
                    mstat[:, c * 24 + q * 6:c * 24 + q * 6 + 6],
                    axC[:, c * 2048 + q * 512:c * 2048 + q * 512 + 512]), "ve")

        for s in range(LT + 4):
            if s == 0:
                p5_A(0, gset=(0, 1))
                p4_recip(C - 1)
                p4_bc(C - 1)
                p4_ndb_evac(C - 1)
                dbg_out("ndb", ndb[:], [("ve", cnt["ve"])])
                p5_A(0, gset=(2,))
            elif s < LT:
                p5_A(s)
            if 1 <= s <= LT:
                p5_B_mm(s - 1)
                p5_B_evac(s - 1)
            if 2 <= s <= LT + 1:
                p5_C_mm(s - 2)
            if 3 <= s <= LT + 2:
                p5_C_ln(s - 3)
            if 2 <= s <= LT + 1:
                p5_CW(s - 2)
            if 4 <= s <= LT + 3:
                p5_D(s - 4)
            if s in (2, 4, 6, 8, 10, 12, 14, 16):
                emit_me_next()
        ve_x2t_done = cnt["ve"]
        sc_x2_done = cnt["sc"]
        pe_p5_done = cnt["pe"]
        dbg_out("x2b", axA[:], [("sc", sc_x2_done)])
        dbg_out("x2Tb", axC[:], [("ve", ve_x2t_done)])
        for e in range(E):
            if e not in me_pe:
                emit_me(e)

        wait("gpsimd", "pe", me_pe[E - 1])
        gd_base = emi("gpsimd", lambda g: g.dma_start(wg[:, 0:8192], wbase), "gb", 16)

        wait("sync", "pe", p5mk["numdone"])
        wait("sync", "sm", me_sd[E - 2])
        sd_mall_a = emi("sync", lambda s: s.dma_start(
            axB[0:65, AXB_MALL:AXB_MALL + 7 * D], mall_dram[:, 0:7 * D]), "sm2", 16)
        wait("sync", "sm", me_sd[E - 1])
        sd_mall = emi("sync", lambda s: s.dma_start(
            axB[0:65, AXB_MALL + 7 * D:AXB_MALL + 8 * D],
            mall_dram[:, 7 * D:8 * D]), "sm3", 16)

        wait("vector", "sc", sc_x2_done)
        wait("vector", "pe", pe_p5_done)
        for c, q in mpend:
            wait("vector", "ve", mq_mark[q])
            emi("vector", lambda v, c=c, q=q: v.bn_stats(
                mstat[:, c * 24 + q * 6:c * 24 + q * 6 + 6],
                axC[:, c * 2048 + q * 512:c * 2048 + q * 512 + 512]), "ve")
        mpend.clear()
        wait("vector", "ve", cnt["ve"])
        for c in range(C):
            emi("vector", lambda v, c=c: v.bn_aggr(
                mcols[:, c * 2:c * 2 + 2], mstat[:, c * 24:c * 24 + 24]),
                "ve" if c == C - 1 else None)
        wait("vector", "ve", cnt["ve"])
        for c in range(C):
            emi("vector", lambda v, c=c: v.tensor_mul(
                mtmp[:, c * 8:c * 8 + 1], mcols[:, c * 2:c * 2 + 1],
                mcols[:, c * 2:c * 2 + 1]), "ve" if c == C - 1 else None)
        wait("vector", "ve", cnt["ve"])
        for c in range(C):
            emi("vector", lambda v, c=c: v.tensor_add(
                mcols[:, c * 2 + 1:c * 2 + 2], mcols[:, c * 2 + 1:c * 2 + 2],
                mtmp[:, c * 8:c * 8 + 1]), "ve" if c == C - 1 else None)
        ve_m = cnt["ve"]
        dbg_out("mcols", mcols[:], [("ve", ve_m)])

        wait("tensor", "gb", gd_base)
        wait("tensor", "sc", sc_x2_done)
        wait("tensor", "ve", ve_x2t_done)
        for oh in range(2):
            bank = pb[oh]
            for kc in range(C):
                lhs = axC[:, kc * 2048: kc * 2048 + 128]
                rhs = wg[:, kc * 1024 + oh * 512: kc * 1024 + oh * 512 + 512]
                emi("tensor", lambda t, o=bank[:, 0:512], a=lhs, b=rhs,
                    st=(kc == 0): t.matmul(o, a, b, start=st, stop=False))

        wait("tensor", "ve", ve_m)
        pe_loss = 0
        for c in range(C):
            emi("tensor", lambda t, a=mcols[:, c * 2 + 1:c * 2 + 2],
                b=discw[:, c * 24:c * 24 + 8], st=(c == 0):
                t.matmul(pb[2][0:1, 0:8], a, b, start=st, stop=False))
            emi("tensor", lambda t, a=mcols[:, c * 2:c * 2 + 1],
                b=discw[:, c * 24 + 8:c * 24 + 16]:
                t.matmul(pb[2][0:1, 0:8], a, b, start=False, stop=False))
            pe_loss = emi("tensor", lambda t, b=discw[:, c * 24 + 16:c * 24 + 24],
                          sp=(c == C - 1):
                          t.matmul(pb[2][0:1, 0:8], onescolf_s[:], b,
                                   start=False, stop=sp), "pe")
        dbg_rawloss = pe_loss

        wait("vector", "pe", pe_loss)
        emi("vector", lambda v: v.tensor_reduce(lossv[0:1, 8:9], pb[2][0:1, 0:8],
                                                AX.X, ALU.min), "ve")
        wait("vector", "ve", cnt["ve"])
        ve_mask = emi("vector", lambda v: v.tensor_scalar(
            lossv[0:1, 0:8], pb[2][0:1, 0:8], lossv[0:1, 8:9], None,
            ALU.is_equal), "ve")
        dbg_out("lossv", lossv[:], [("ve", ve_mask)])
        wait("tensor", "ve", ve_mask)
        pe_mt = emi("tensor", lambda t: t.transpose(pb[5][0:8, 0:1],
                                                    lossv[0:1, 0:8],
                                                    ones1f_s[0:1, 0:1]), "pe")
        pe_mb = emi("tensor", lambda t: t.matmul(pb[6][:, 0:8], ones1f_s[:],
                                                 lossv[0:1, 0:8],
                                                 start=True, stop=True), "pe")
        wait("vector", "pe", pe_mb)
        emi("vector", lambda v: v.tensor_copy(maskbc[:], pb[6][:, 0:8]), "ve")
        ve_mbc = emi("vector", lambda v: v.tensor_copy(mask8[:], pb[5][0:8, 0:1]),
                     "ve")
        dbg_out("maskbc", maskbc[:], [("ve", ve_mbc)])

        wait("tensor", "ve", ve_mbc)
        wait("tensor", "g0", g0_all)
        emi("tensor", lambda t: t.matmul(pb[5][0:1, 0:512], mask8[:],
                                         convb_s[:, 0:512], start=True, stop=True))
        pe_bc2 = emi("tensor", lambda t: t.matmul(pb[6][0:1, 0:512], mask8[:],
                                                  convb_s[:, 512:1024],
                                                  start=True, stop=True), "pe")

        wait("vector", "ve", ve_mbc)
        mid128 = scr
        for e in range(E):
            emi("vector", lambda v, e=e: v.tensor_scalar(
                mid128[:, e * 128:e * 128 + 128], identb_s[:],
                maskbc[:, e:e + 1], None, ALU.mult),
                "ve" if e == E - 1 else None)
        ve_mid = cnt["ve"]
        emi("vector", lambda v: v.tensor_mul(
            mtmp[0:64, 0:8], smallsb[0:64, 144:152], maskbc[0:64, 0:8]), "ve")
        wait("vector", "ve", cnt["ve"])
        ve_bd = emi("vector", lambda v: v.tensor_reduce(
            bdcol[:], mtmp[0:64, 0:8], AX.X, ALU.add), "ve")

        wait("tensor", "ga", gd_dw)
        wait("tensor", "ve", ve_mid)
        wait("tensor", "sc", sc_x2_done)
        for c in range(C):
            for e in range(E):
                dwe = axD[:, AXD_DW + c * 512 + e * 64:AXD_DW + c * 512 + e * 64 + 64]
                emi("tensor", lambda t, o=pb[3][:, c * 64:c * 64 + 64],
                    a=mid128[:, e * 128:e * 128 + 128], b=dwe,
                    st=(e == 0), sp=(e == E - 1):
                    t.matmul(o, a, b, start=st, stop=sp),
                    "pe" if (c == C - 1 and e == E - 1) else None)
        pe_wd = cnt["pe"]
        wait("tensor", "sm2", sd_mall_a)
        msel_pe = {}
        wait("tensor", "sc", me_sc[E - 1])
        for oh in range(2):
            bank = pb[4] if oh == 0 else pb[7]
            for e in range(E):
                if oh == 0 and e == E - 1:
                    wait("tensor", "sm3", sd_mall)
                mall_e = axB[0:65, AXB_MALL + e * 1024 + oh * 512:
                             AXB_MALL + e * 1024 + oh * 512 + 512]
                emi("tensor", lambda t, o=bank[0:65, 0:512],
                    a=mid128[0:65, e * 128:e * 128 + 65], b=mall_e,
                    st=(e == 0), sp=(e == E - 1):
                    t.matmul(o, a, b, start=st, stop=sp),
                    "pe" if e == E - 1 else None)
            msel_pe[oh] = cnt["pe"]
        wait("scalar", "pe", pe_wd)
        sc_wd = emi("scalar", lambda sc: sc.activation(
            axB[:, AXB_WDSEL:AXB_WDSEL + 512], pb[3][:, 0:512],
            AF.Identity, bias=0.0, scale=1.0), "sc")
        wait("scalar", "pe", msel_pe[1])
        emi("scalar", lambda sc: sc.activation(
            axB[0:65, AXB_MSEL:AXB_MSEL + 512], pb[4][0:65, 0:512],
            AF.Identity, bias=0.0, scale=1.0), "sc")
        sc_msel = emi("scalar", lambda sc: sc.activation(
            axB[0:65, AXB_MSEL + 512:AXB_MSEL + 1024], pb[7][0:65, 0:512],
            AF.Identity, bias=0.0, scale=1.0), "sc")
        dbg_out("wdsel", axB[:, AXB_WDSEL:AXB_WDSEL + 512], [("sc", sc_wd)])

        rowstg = axB[0:1, AXB_ROW:AXB_ROW + 1024]
        wait("vector", "pe", pe_bc2)
        emi("vector", lambda v: v.tensor_add(rowstg[0:1, 0:512], pb[5][0:1, 0:512],
                                             rows3b[0:1, 2048:2560]))
        ve_br = emi("vector", lambda v: v.tensor_add(rowstg[0:1, 512:1024],
                                             pb[6][0:1, 0:512],
                                             rows3b[0:1, 2560:3072]), "ve")
        ve_hone = emi("vector", lambda v: v.memset(
            axB[64:65, AXB_H:AXB_H + 2048], 1.0), "ve")
        dbg_out("msel", axB[0:65, AXB_MSEL:AXB_MSEL + 1024], [("ve", ve_br)])

        wait("tensor", "sc", sc_wd)
        wait("tensor", "sc", me_sc[E - 1])
        wait("scalar", "ve", ve_bd)
        h_sc = {}
        for ltb in range(4):
            if ltb - 1 in h_sc:
                wait("tensor", "sc", h_sc[ltb - 1])
            for kc in range(C):
                lhs = axB[:, AXB_WDSEL + kc * 64:AXB_WDSEL + kc * 64 + 64]
                rhs = axC[:, kc * 2048 + ltb * 512: kc * 2048 + ltb * 512 + 512]
                if kc == C - 1:
                    pe_h = emi("tensor", lambda t, o=pb[3][0:64, 0:512], a=lhs, b=rhs:
                               t.matmul(o, a, b, start=False, stop=True), "pe")
                else:
                    emi("tensor", lambda t, o=pb[3][0:64, 0:512], a=lhs, b=rhs,
                        st=(kc == 0): t.matmul(o, a, b, start=st, stop=False))
            wait("scalar", "pe", pe_h)
            h_sc[ltb] = emi("scalar", lambda sc, ltb=ltb: sc.activation(
                axB[0:64, AXB_H + ltb * 512:AXB_H + ltb * 512 + 512],
                pb[3][0:64, 0:512], AF.Relu, bias=bdcol[:], scale=1.0), "sc")
        sc_h_done = cnt["sc"]
        dbg_out("hbuf", axB[0:65, AXB_H:AXB_H + 2048], [("sc", sc_h_done)])

        wait("tensor", "gb", gd_base)
        wait("tensor", "sc", sc_msel)
        wait("tensor", "ve", ve_br)
        wait("tensor", "ve", ve_hone)
        wait("tensor", "sc", sc_h_done)
        ffn_pe, o_sc, o_sd = {}, {}, {}
        for lt in range(LT):
            b0 = pb[0 + 2 * (lt % 2)]
            b1 = pb[1 + 2 * (lt % 2)]
            ls = lnst[:, (lt % 2) * 20:(lt % 2) * 20 + 20]
            nmr = ls[:, 16:17]
            rstd = ls[:, 15:16]
            if lt - 2 in o_sc:
                wait("tensor", "sc", o_sc[lt - 2])
            for oh in range(2):
                bank = b0 if oh == 0 else b1
                if lt > 0:
                    for kc in range(C):
                        lhs = axC[:, kc * 2048 + lt * 128:
                                  kc * 2048 + lt * 128 + 128]
                        rhs = wg[:, kc * 1024 + oh * 512:
                                 kc * 1024 + oh * 512 + 512]
                        emi("tensor", lambda t, o=bank[:, 0:512], a=lhs, b=rhs,
                            st=(kc == 0): t.matmul(o, a, b, start=st, stop=False))
                emi("tensor", lambda t, o=bank[:, 0:512], a=onesrow_s[:],
                    b=rowstg[0:1, oh * 512:oh * 512 + 512]:
                    t.matmul(o, a, b, start=False, stop=False))
                ffn_pe[(lt, oh)] = emi("tensor", lambda t, o=bank[:, 0:512],
                                       a=axB[0:65, AXB_H + lt * 128:
                                             AXB_H + lt * 128 + 128],
                                       b=axB[0:65, AXB_MSEL + oh * 512:
                                             AXB_MSEL + oh * 512 + 512]:
                                       t.matmul(o, a, b, start=False, stop=True), "pe")
            wait("vector", "pe", ffn_pe[(lt, 1)])
            emi("vector", lambda v, o=b0[:, 0:512], a=b0[:, 0:512],
                b=axA[:, lt * 1024:lt * 1024 + 512]: v.tensor_add(o, a, b))
            emi("vector", lambda v, o=b1[:, 0:512], a=b1[:, 0:512],
                b=axA[:, lt * 1024 + 512:lt * 1024 + 1024]: v.tensor_add(o, a, b),
                "ve")
            wait("vector", "ve", cnt["ve"])
            emi("vector", lambda v, o=ls[:, 0:6], a=b0[:, 0:512]: v.bn_stats(o, a))
            emi("vector", lambda v, o=ls[:, 6:12], a=b1[:, 0:512]: v.bn_stats(o, a),
                "ve")
            wait("vector", "ve", cnt["ve"])
            ve_bn = emi("vector", lambda v, o=ls[:, 12:14], a=ls[:, 0:12]:
                        v.bn_aggr(o, a), "ve")
            wait("scalar", "ve", ve_bn)
            emi("scalar", lambda sc, o=ls[:, 14:15], a=ls[:, 13:14]:
                sc.activation(o, a, AF.Ln, bias=epscol[:], scale=1.0), "sc")
            wait("scalar", "sc", cnt["sc"])
            sc_r = emi("scalar", lambda sc, o=rstd, a=ls[:, 14:15]:
                       sc.activation(o, a, AF.Exp, bias=0.0, scale=-0.5), "sc")
            wait("vector", "sc", sc_r)
            ve_nmr = emi("vector", lambda v, o=nmr, a=ls[:, 12:13], b=rstd:
                         v.scalar_tensor_tensor(o, a, -1.0, b, ALU.mult, ALU.mult),
                         "ve")
            wait("scalar", "ve", ve_nmr)
            if lt - 2 in o_sd:
                wait("scalar", "so0" if lt % 2 == 0 else "so1", o_sd[lt - 2])
            emi("scalar", lambda sc, o=iostg[:, (lt % 2) * 1024:(lt % 2) * 1024 + 512],
                a=b0[:, 0:512], b=nmr, r=rstd:
                sc.activation(o, a, AF.Identity, bias=b, scale=r))
            o_sc[lt] = emi("scalar", lambda sc,
                           o=iostg[:, (lt % 2) * 1024 + 512:(lt % 2) * 1024 + 1024],
                           a=b1[:, 0:512], b=nmr, r=rstd:
                           sc.activation(o, a, AF.Identity, bias=b, scale=r), "sc")
            wait("sync", "sc", o_sc[lt])
            o_sd[lt] = emi("sync", lambda s, lt=lt: s.dma_start(
                out_t[lt], iostg[:, (lt % 2) * 1024:(lt % 2) * 1024 + 1024]),
                "so0" if lt % 2 == 0 else "so1", 16)

        wait("sync", "so0", cnt["so0"])
        wait("sync", "so1", cnt["so1"])
        if cnt["sd"]:
            wait("sync", "sd", cnt["sd"])

        blk = ctx.enter_context(nc.Block())

        @blk.tensor
        def _(t):
            for f in opsq["tensor"]:
                f(t)

        @blk.scalar
        def _(s):
            for f in opsq["scalar"]:
                f(s)

        @blk.vector
        def _(v):
            for f in opsq["vector"]:
                f(v)

        @blk.sync
        def _(s):
            for f in opsq["sync"]:
                f(s)

        @blk.gpsimd
        def _(g):
            for f in opsq["gpsimd"]:
                f(g)

    return nc


def _chunkT(wT):
    m = wT.shape[1]
    return np.ascontiguousarray(
        wT.reshape(8, 128, m).transpose(1, 0, 2).reshape(128, 8 * m))


def prep_inputs(inp):
    f32 = np.float32
    Wq = np.asarray(inp["L0_attn_Wq"], f32)
    Wk = np.asarray(inp["L0_attn_Wk"], f32)
    Wv = np.asarray(inp["L0_attn_Wv"], f32)
    Wo = np.asarray(inp["L0_attn_Wo"], f32)
    Wrf = np.asarray(inp["L0_attn_Wrf"], f32)
    baseW = np.asarray(inp["L0_ffn_base_W"], f32)
    x = np.asarray(inp["L0_attn_x"], f32)

    shared = {}
    shared["wq"] = _chunkT(Wq.T)
    shared["wk"] = np.ascontiguousarray(
        Wk.T.reshape(8, 128, 8, 128).transpose(1, 2, 0, 3).reshape(128, 8192))
    shared["wv"] = _chunkT(Wv.T)
    shared["wo"] = _chunkT(Wo.T)
    shared["wbase"] = _chunkT(baseW.T)

    W0 = Wrf[0]
    blk = np.zeros((8, 128, 128), f32)
    for c in range(8):
        blk[c, 0:64, 0:64] = W0[2 * c]
        blk[c, 64:128, 64:128] = W0[2 * c + 1]
    shared["w0pair"] = np.ascontiguousarray(
        blk.transpose(1, 0, 2).reshape(128, 1024))

    downW = np.asarray(inp["L0_ffn_down_W"], f32)
    shared["downw"] = np.ascontiguousarray(
        downW.reshape(8, 8, 128, 64).transpose(2, 1, 0, 3).reshape(128, 4096))

    upW = np.asarray(inp["L0_ffn_up_W"], f32)
    upb = np.asarray(inp["L0_ffn_up_b"], f32)
    aug = np.concatenate([upW.transpose(0, 2, 1),
                          upb[:, :, None]], axis=2)
    shared["wuaug"] = np.ascontiguousarray(
        aug.reshape(8, 8, 128, 65).transpose(2, 0, 1, 3).reshape(128, 8 * 8 * 65))

    convW = np.asarray(inp["L0_ffn_conv_W"], f32)
    shared["wct"] = np.ascontiguousarray(
        convW.transpose(0, 2, 1).reshape(8, 8, 128, 1024)
        .transpose(2, 0, 1, 3).reshape(128, 8 * 8192))

    bq = np.asarray(inp["L0_attn_bq"], f32)
    bk = np.asarray(inp["L0_attn_bk"], f32)
    mu = np.asarray(inp["L0_ffn_disc_mu"], f32)
    sg = np.asarray(inp["L0_ffn_disc_sigma"], f32)
    bd = np.asarray(inp["L0_ffn_down_b"], f32)
    smalls = np.zeros((128, 152), f32)
    smalls[:, 0:8] = bq.reshape(8, 128).T
    smalls[:, 8:16] = bk.reshape(8, 128).T
    smalls[:, 16:80] = mu.T.reshape(8, 128, 8).transpose(1, 0, 2).reshape(128, 64)
    smalls[:, 80:144] = sg.T.reshape(8, 128, 8).transpose(1, 0, 2).reshape(128, 64)
    smalls[0:64, 144:152] = bd.T
    shared["smalls"] = smalls

    bv = np.asarray(inp["L0_attn_bv"], f32)
    bo = np.asarray(inp["L0_attn_bo"], f32)
    bb = np.asarray(inp["L0_ffn_base_b"], f32)
    shared["rows3"] = np.concatenate([bv, bo, bb])[None, :].astype(f32)
    shared["convb"] = np.asarray(inp["L0_ffn_conv_b"], f32)

    shared["identb"] = np.eye(128, dtype=f32)
    nm = np.zeros((128, 128), f32)
    nm[0:64, 0:64] = -0.5
    nm[64:128, 64:128] = -0.5
    shared["negmask"] = nm
    shared["onesrow_b"] = np.ones((1, 128), f32)
    shared["onescol_b"] = np.ones((128, 1), f32)
    shared["ones1f"] = np.ones((1, 128), f32)
    shared["onescolf"] = np.ones((128, 1), f32)
    b2 = np.zeros((128, 2), f32)
    b2[0:64, 0] = 1.0
    b2[64:128, 1] = 1.0
    shared["blockones2"] = b2
    i2 = np.zeros((2, 128), f32)
    i2[0, 0:64] = 1.0
    i2[1, 64:128] = 1.0
    shared["indic2"] = i2

    in_maps = []
    for b in range(NCORES):
        m = dict(shared)
        m["xT"] = np.ascontiguousarray(
            x[b].T.reshape(8, 128, 4, 512).transpose(1, 2, 0, 3).reshape(128, 16384))
        m["xtok"] = np.ascontiguousarray(x[b])
        in_maps.append(m)
    return in_maps


def kernel(**inputs):
    in_maps = prep_inputs(inputs)
    nc = build_program(DBG)
    res = run_bass_kernel_spmd(nc, in_maps, core_ids=list(range(NCORES)))
    return np.stack([np.asarray(r["out"], np.float32) for r in res.results], axis=0)


# version 36
# speedup vs baseline: 1.0384x; 1.0344x over previous
from contextlib import ExitStack
import numpy as np
import ml_dtypes
import concourse.bass as bass
import concourse.mybir as mybir
from concourse.bass_utils import run_bass_kernel_spmd

F32 = mybir.dt.float32
BF16 = mybir.dt.bfloat16
AF = mybir.ActivationFunctionType
ALU = mybir.AluOpType
AX = mybir.AxisListType

NCORES = 8
L = 2048
D = 1024
C = 8
LT = 16
E = 8
EPS_ATTN = 1e-6
EPS_LN = 1e-5

SCR_WU = 0
SCR_K2 = 1088
SCR_EK = 3136
SCR_MSTG = 7232

AXD_CTX = 0
AXD_CTXT = 2048
AXD_ZF = 2048
AXD_DW = 8192

AXB_MALL = 0
AXB_H = 8192
AXB_MSEL = 10240
AXB_WDSEL = 11264
AXB_ROW = 11776

DBG = ()
SCHED = []


def build_program(dbg=()):
    dbg = set(dbg)
    nc = bass.Bass("TRN2", target_bir_lowering=False)

    def din(name, shape, dt=F32):
        return nc.dram_tensor(name, shape, dt, kind="ExternalInput").ap()

    xT = din("xT", [128, C * L])
    xtok = din("xtok", [L, D])
    xtok_t = xtok.rearrange("(t p) d -> t p d", p=128)
    wq = din("wq", [128, C * D])
    wk = din("wk", [128, C * D])
    wv = din("wv", [128, C * D])
    wo = din("wo", [128, C * D])
    wbase = din("wbase", [128, C * D])
    w0pair = din("w0pair", [128, C * 128])
    downw = din("downw", [128, C * E * 64])
    wuaug = din("wuaug", [128, E * C * 65])
    wct = din("wct", [128, E * C * D])
    smalls = din("smalls", [128, 152])
    rows3 = din("rows3", [1, 3 * D])
    convb = din("convb", [E, D])
    identb = din("identb", [128, 128])
    negmask = din("negmask", [128, 128])
    onesrow_b = din("onesrow_b", [1, 128])
    onescol_b = din("onescol_b", [128, 1])
    ones1f = din("ones1f", [1, 128], F32)
    onescolf = din("onescolf", [128, 1], F32)
    blockones2 = din("blockones2", [128, 2], F32)
    indic2 = din("indic2", [2, 128], F32)

    out = nc.dram_tensor("out", [L, D], F32, kind="ExternalOutput").ap()
    out_t = out.rearrange("(t p) d -> t p d", p=128)
    mall_dram = nc.dram_tensor("mall_scratch", [65, E * D], BF16).ap()

    with ExitStack() as ctx:
        def sb(name, shape, dt=BF16):
            return ctx.enter_context(nc.sbuf_tensor(name, shape, dt))

        def pst(name):
            return ctx.enter_context(nc.psum_tensor(name, [128, 512], F32))

        axA = sb("axA", [128, 16384])
        axB = sb("axB", [128, 16384])
        axC = sb("axC", [128, 16384])
        axD = sb("axD", [128, 16384])
        scr = sb("scr", [128, 8256])
        wg = sb("wg", [128, 16384])
        iostg = sb("iostg", [128, 2048], F32)
        xstg = sb("xstg", [128, 2048])
        ndb = sb("ndb", [128, C * 130])
        smallsb = sb("smallsb", [128, 152], F32)
        rows3b = sb("rows3b", [1, 3 * D], BF16)
        identb_s = sb("identb_s", [128, 128])
        negmask_s = sb("negmask_s", [128, 128])
        w0pb = sb("w0pb", [128, C * 128])
        onesrow_s = sb("onesrow_s", [1, 128])
        onescol_s = sb("onescol_s", [128, 1])
        ones1f_s = sb("ones1f_s", [1, 128], F32)
        onescolf_s = sb("onescolf_s", [128, 1], F32)
        blk2_s = sb("blk2_s", [128, 2], F32)
        indic2_s = sb("indic2_s", [2, 128], F32)
        convb_s = sb("convb_s", [E, D])
        bvbc = sb("bvbc", [128, D])
        discw = sb("discw", [128, C * 24], F32)
        mtmp = sb("mtmp", [128, C * 8], F32)
        mstat = sb("mstat", [128, C * 24], F32)
        mcols = sb("mcols", [128, C * 2], F32)
        lossv = sb("lossv", [1, 16], F32)
        mask8 = sb("mask8", [8, 1])
        maskbc = sb("maskbc", [128, 8], F32)
        invz = sb("invz", [2, 1], F32)
        dcol = sb("dcol", [128, 1], F32)
        bdcol = sb("bdcol", [64, 1], F32)
        lnst = sb("lnst", [128, 40], F32)
        ddbuf = sb("ddbuf", [128, 24], F32)
        epscol = sb("epscol", [128, 1], F32)
        mxb = sb("mxb", [128, 24], F32)

        axDf = axD.bitcast(F32)
        axBf = axB.bitcast(F32)

        pb = [pst(f"pb{i}") for i in range(8)]
        pbb = [p.bitcast(BF16) for p in pb]

        SEM = {}
        for k in ("pe", "sc", "ve", "sd", "gp", "g0", "ga", "gb",
                  "gc", "gw", "g1", "gx", "xq1", "xq2", "xq3", "g2", "s0",
                  "sx0", "sx1", "sx2", "sx3", "so0", "so1", "sm", "sm2", "sm3"):
            SEM[k] = ctx.enter_context(nc.semaphore("s_" + k))
        cnt = dict(pe=0, sc=0, ve=0, sd=0, gp=0, g0=0, ga=0, gb=0,
                   gc=0, gw=0, g1=0, gx=0, xq1=0, xq2=0, s0=0, sx0=0, sx1=0,
                   so0=0, so1=0, sm=0, sm2=0, sm3=0, xq3=0, g2=0, sx2=0, sx3=0)
        opsq = {k: [] for k in ("tensor", "scalar", "vector", "sync", "gpsimd")}

        def em(eng, fn):
            opsq[eng].append(fn)

        SCHED.clear()

        def wait(eng, key, v):
            if v and v > 0:
                SCHED.append(("wait", eng, key, v))
                em(eng, lambda e, s=SEM[key], v=v: e.wait_ge(s, v))

        def emi(eng, build, inckey=None, n=1):
            if inckey is None:
                SCHED.append(("op", eng, None, 0))
                em(eng, build)
                return None
            cnt[inckey] += n
            SCHED.append(("op", eng, inckey, n))
            em(eng, lambda e, b=build, s=SEM[inckey], n=n: b(e).then_inc(s, n))
            return cnt[inckey]



        def dbg_out(name, src_ap, gates):
            if name not in dbg:
                return
            ap = nc.dram_tensor("dbg_" + name, list(src_ap.shape), src_ap.dtype,
                                kind="ExternalOutput").ap()
            for k, v in gates:
                wait("sync", k, v)
            emi("sync", lambda s, a=ap, b=src_ap: s.dma_start(a, b), "sd", 16)

        for dst, src in ((ones1f_s, ones1f), (onescolf_s, onescolf),
                         (blk2_s, blockones2), (indic2_s, indic2),
                         (smallsb, smalls)):
            emi("sync", lambda s, d=dst, x=src: s.dma_start(d[:], x), "s0", 16)
        sd_consts = cnt["s0"]
        for dst, src in ((identb_s, identb), (negmask_s, negmask),
                         (onesrow_s, onesrow_b), (onescol_s, onescol_b)):
            emi("gpsimd", lambda g, d=dst, x=src: g.dma_start(d[:], x), "g0", 16)

        emi("gpsimd", lambda g: g.dma_start(wg[:, 0:4096], wk[:, 0:4096]), "g1", 16)
        emi("gpsimd", lambda g: g.dma_start(axA[:, 0:4096], xT[:, 0:4096]), "g1", 16)
        wait("gpsimd", "g1", cnt["g1"])
        emi("gpsimd", lambda g: g.dma_start(wg[:, 4096:8192], wk[:, 4096:8192]),
            "gx", 16)
        wait("gpsimd", "gx", cnt["gx"])
        emi("gpsimd", lambda g: g.dma_start(axA[:, 4096:8192], xT[:, 4096:8192]),
            "xq1", 16)
        wait("gpsimd", "xq1", cnt["xq1"])
        emi("gpsimd", lambda g: g.dma_start(axA[:, 8192:12288], xT[:, 8192:12288]),
            "xq2", 16)
        emi("gpsimd", lambda g: g.dma_start(axA[:, 12288:16384], xT[:, 12288:16384]),
            "xq3", 16)
        emi("gpsimd", lambda g: g.dma_start(wg[:, 8192:16384], wv), "gb", 16)
        emi("gpsimd", lambda g: g.dma_start(w0pb[:], w0pair), "g2", 16)
        emi("gpsimd", lambda g: g.dma_start(rows3b[:], rows3), "g2", 16)
        emi("gpsimd", lambda g: g.dma_start(convb_s[:], convb), "g2", 16)
        g0_all = cnt["g0"]
        g1_all = cnt["g1"]
        gx_all = cnt["gx"]
        g2_all = cnt["g2"]
        xq1_all = cnt["xq1"]
        xq2_all = cnt["xq2"]
        xq3_all = cnt["xq3"]
        gd_wv = cnt["gb"]

        from concourse.hw_specs import get_activation_tables
        nlx_idx = list(get_activation_tables(nc.m.arch).keys()).index(
            "natural_log_exp_and_others")

        def _preload(sc):
            return sc.add_instruction(mybir.InstLoadActFuncSet(
                name=nc.get_next_instruction_name(),
                act_func_set_id=nlx_idx, ins=[], outs=[]))
        em("scalar", _preload)

        ve_eps = emi("vector", lambda v: v.memset(epscol[:], EPS_LN), "ve")

        wait("vector", "s0", sd_consts)
        for step in range(6):
            if step:
                wait("vector", "ve", cnt["ve"])
            for c in range(C):
                mu = smallsb[:, 16 + c * 8:24 + c * 8]
                sg = smallsb[:, 80 + c * 8:88 + c * 8]
                ivd = discw[:, c * 24:c * 24 + 8]
                n2 = discw[:, c * 24 + 8:c * 24 + 16]
                m2w = discw[:, c * 24 + 16:c * 24 + 24]
                mi = mtmp[:, c * 8:c * 8 + 8]
                ik = "ve" if c == C - 1 else None
                if step == 0:
                    emi("vector", lambda v, o=ivd, a=sg: v.tensor_mul(o, a, a), ik)
                elif step == 1:
                    emi("vector", lambda v, o=ivd: v.reciprocal(o, o), ik)
                elif step == 2:
                    emi("vector", lambda v, o=mi, a=mu, b=ivd: v.tensor_mul(o, a, b),
                        ik)
                elif step == 3:
                    emi("vector", lambda v, o=m2w, a=mu, b=mi: v.tensor_mul(o, a, b),
                        ik)
                elif step == 4:
                    emi("vector", lambda v, o=n2, a=mi: v.tensor_scalar(
                        o, a, -2.0 / 1024.0, None, ALU.mult), ik)
                else:
                    emi("vector", lambda v, o=ivd, a=ivd: v.tensor_scalar(
                        o, a, 1.0 / 1024.0, None, ALU.mult), ik)
        wait("vector", "ve", cnt["ve"])
        for c in range(C):
            m2w = discw[:, c * 24 + 16:c * 24 + 24]
            emi("vector", lambda v, o=m2w, a=m2w: v.tensor_scalar(
                o, a, 1.0 / 1024.0, None, ALU.mult), "ve" if c == C - 1 else None)
        ve_disc = cnt["ve"]

        ve_ndbinit = emi("vector", lambda v: v.memset(ndb[:], 0.0), "ve")

        wait("tensor", "g1", g1_all)
        wait("tensor", "s0", sd_consts)
        wait("scalar", "s0", sd_consts)
        kmark = {}
        pe_last = 0
        for i in range(32):
            ltb, mc = divmod(i, 8)
            bank = pb[i % 2]
            if i - 2 in kmark:
                wait("tensor", "sc", kmark[i - 2])
            if i == 4:
                wait("tensor", "gx", gx_all)
            if i == 8:
                wait("tensor", "xq1", xq1_all)
            if i == 16:
                wait("tensor", "xq2", xq2_all)
            if i == 24:
                wait("tensor", "xq3", xq3_all)
            for kc in range(C):
                lhs = wg[:, mc * 1024 + kc * 128: mc * 1024 + kc * 128 + 128]
                rhs = axA[:, ltb * 4096 + kc * 512: ltb * 4096 + kc * 512 + 512]
                if kc == C - 1:
                    pe_last = emi("tensor", lambda t, o=bank[:, 0:512], a=lhs, b=rhs:
                                  t.matmul(o, a, b, start=False, stop=True), "pe")
                else:
                    emi("tensor", lambda t, o=bank[:, 0:512], a=lhs, b=rhs,
                        st=(kc == 0): t.matmul(o, a, b, start=st, stop=False))
            wait("scalar", "pe", pe_last)
            dst = axD[:, mc * 2048 + ltb * 512: mc * 2048 + ltb * 512 + 512]
            bap = smallsb[:, 8 + mc:9 + mc]
            kmark[i] = emi("scalar", lambda sc, o=dst, i_=bank[:, 0:512], b=bap:
                           sc.activation(o, i_, AF.Identity, bias=b, scale=1.0), "sc")
        pe_k_done = pe_last
        dbg_out("kTb", axD[:], [("sc", cnt["sc"])])

        wait("tensor", "g0", g0_all)
        wait("tensor", "g2", g2_all)
        emi("tensor", lambda t: t.matmul(pb[7][:, 0:512], onesrow_s[:],
                                         rows3b[0:1, 0:512], start=True, stop=True))
        pe_bv = emi("tensor", lambda t: t.matmul(pb[6][:, 0:512], onesrow_s[:],
                                                 rows3b[0:1, 512:1024],
                                                 start=True, stop=True), "pe")
        wait("vector", "pe", pe_bv)
        emi("vector", lambda v: v.tensor_copy(bvbc[:, 0:512], pb[7][:, 0:512]))
        ve_bvbc = emi("vector", lambda v: v.tensor_copy(bvbc[:, 512:1024],
                                                        pb[6][:, 0:512]), "ve")

        wait("gpsimd", "pe", pe_k_done)
        gd_wq = emi("gpsimd", lambda g: g.dma_start(wg[:, 0:8192], wq), "ga", 16)

        wait("tensor", "gb", gd_wv)
        vmark = {}
        for i in range(32):
            lt, oh = divmod(i, 2)
            bank = pb[2 + i % 2]
            if i - 2 in vmark:
                wait("tensor", "ve", vmark[i - 2])
            for kc in range(C):
                lhs = axA[:, (lt // 4) * 4096 + kc * 512 + (lt % 4) * 128:
                          (lt // 4) * 4096 + kc * 512 + (lt % 4) * 128 + 128]
                rhs = wg[:, 8192 + kc * 1024 + oh * 512:
                         8192 + kc * 1024 + oh * 512 + 512]
                if kc == C - 1:
                    pe_last = emi("tensor", lambda t, o=bank[:, 0:512], a=lhs, b=rhs:
                                  t.matmul(o, a, b, start=False, stop=True), "pe")
                else:
                    emi("tensor", lambda t, o=bank[:, 0:512], a=lhs, b=rhs,
                        st=(kc == 0): t.matmul(o, a, b, start=st, stop=False))
            wait("vector", "pe", pe_last)
            if i == 0:
                wait("vector", "ve", ve_bvbc)
            dst = axC[:, lt * 1024 + oh * 512: lt * 1024 + oh * 512 + 512]
            vmark[i] = emi("vector", lambda v, o=dst, a=bank[:, 0:512],
                           b=bvbc[:, oh * 512:oh * 512 + 512]:
                           v.tensor_add(o, a, b), "ve")
        pe_v_done = pe_last
        ve_v_done = cnt["ve"]
        dbg_out("vb", axC[:], [("ve", ve_v_done)])

        wait("gpsimd", "pe", pe_v_done)
        gd_wo = emi("gpsimd", lambda g: g.dma_start(wg[:, 8192:16384], wo), "gb", 16)

        wait("tensor", "ga", gd_wq)
        wait("tensor", "ve", ve_v_done)
        wait("tensor", "sc", kmark[31])
        qmark, sqsc, trve = {}, {}, {}

        def p3_q(i):
            c, ltb = divmod(i, 4)
            qbank = pb[i % 2]
            qslot = scr[:, (i % 2) * 512:(i % 2) * 512 + 512]
            if i - 2 in qmark:
                wait("tensor", "ve", qmark[i - 2])
            for kc in range(C):
                lhs = wg[:, kc * 1024 + c * 128: kc * 1024 + c * 128 + 128]
                rhs = axA[:, ltb * 4096 + kc * 512: ltb * 4096 + kc * 512 + 512]
                if kc == C - 1:
                    pe_last = emi("tensor", lambda t, o=qbank[:, 0:512], a=lhs, b=rhs:
                                  t.matmul(o, a, b, start=False, stop=True), "pe")
                else:
                    emi("tensor", lambda t, o=qbank[:, 0:512], a=lhs, b=rhs,
                        st=(kc == 0): t.matmul(o, a, b, start=st, stop=False))
            wait("vector", "pe", pe_last)
            bap = smallsb[:, c:c + 1]
            qmark[i] = emi("vector", lambda v, o=qslot, a=qbank[:, 0:512], b=bap:
                           v.tensor_scalar(o, a, b, None, ALU.add), "ve")

        def p3_sq(i):
            c, ltb = divmod(i, 4)
            sbank = pb[2 + i % 2]
            qslot = scr[:, (i % 2) * 512:(i % 2) * 512 + 512]
            estage = scr[:, SCR_EK + (i % 3) * 512: SCR_EK + (i % 3) * 512 + 512]
            mxs = mxb[:, (i % 3) * 8:(i % 3) * 8 + 8]
            if i - 2 in sqsc:
                wait("tensor", "sc", sqsc[i - 2])
            wait("tensor", "ve", qmark[i])
            for j in range(4):
                oc = sbank[:, j * 128:j * 128 + 128]
                lhs = qslot[:, j * 128:j * 128 + 128]
                if j == 3:
                    pe_sq = emi("tensor", lambda t, o=oc, a=lhs,
                                b=w0pb[:, c * 128:c * 128 + 128]:
                                t.matmul(o, a, b, start=True, stop=True), "pe")
                else:
                    emi("tensor", lambda t, o=oc, a=lhs,
                        b=w0pb[:, c * 128:c * 128 + 128]:
                        t.matmul(o, a, b, start=True, stop=True))
            wait("vector", "pe", pe_sq)
            s3 = sbank[:].rearrange("p (j h f) -> p j h f", j=4, h=2)
            emi("vector", lambda v, o=mxs[:, 0:4], a=s3[:, :, 0, :]:
                v.tensor_reduce(o, a, AX.X, ALU.max, negate=True))
            ve_mx = emi("vector", lambda v, o=mxs[:, 4:8], a=s3[:, :, 1, :]:
                        v.tensor_reduce(o, a, AX.X, ALU.max, negate=True), "ve")
            wait("scalar", "ve", ve_mx)
            for j in range(4):
                emi("scalar", lambda sc, o=estage[:, j * 128:j * 128 + 64],
                    i_=sbank[:, j * 128:j * 128 + 64], b=mxs[:, j:j + 1]:
                    sc.activation(o, i_, AF.Exp, bias=b, scale=1.0))
                sqsc[i] = emi("scalar", lambda sc,
                              o=estage[:, j * 128 + 64:j * 128 + 128],
                              i_=sbank[:, j * 128 + 64:j * 128 + 128],
                              b=mxs[:, 4 + j:5 + j]:
                              sc.activation(o, i_, AF.Exp, bias=b, scale=1.0), "sc")

        def p3_tr(i):
            c, ltb = divmod(i, 4)
            tbank = pbb[6 + i % 2]
            estage = scr[:, SCR_EK + (i % 3) * 512: SCR_EK + (i % 3) * 512 + 512]
            wait("tensor", "sc", sqsc[i])
            if i - 2 in trve:
                wait("tensor", "ve", trve[i - 2])
            for j in range(4):
                if j == 3:
                    pe_tr = emi("tensor", lambda t,
                                o=tbank[:, j * 128:j * 128 + 128],
                                a=estage[:, j * 128:j * 128 + 128]:
                                t.transpose(o, a, identb_s[:]), "pe")
                else:
                    emi("tensor", lambda t, o=tbank[:, j * 128:j * 128 + 128],
                        a=estage[:, j * 128:j * 128 + 128]:
                        t.transpose(o, a, identb_s[:]))
            wait("vector", "pe", pe_tr)
            trve[i] = emi("vector", lambda v,
                          o=axB[:, c * 2048 + ltb * 512:c * 2048 + ltb * 512 + 512],
                          a=tbank[:, 0:512]: v.tensor_copy(o, a), "ve")

        for s in range(34):
            if s < 32:
                p3_q(s)
            if 1 <= s <= 32:
                p3_sq(s - 1)
            if s >= 2:
                p3_tr(s - 2)
        pe_q_done = cnt["pe"]
        sc_eq_done = cnt["sc"]
        ve_eq_done = cnt["ve"]
        dbg_out("EqT", axB[:], [("ve", ve_eq_done)])

        wct_gd, wu_gd, me_pe, me_sc, me_sd = {}, {}, {}, {}, {}

        def sched_wct_dma(e, gate_pe):
            if gate_pe:
                wait("gpsimd", "pe", gate_pe)
            wct_gd[e] = emi("gpsimd", lambda g, e=e: g.dma_start(
                wg[:, 0:8192], wct[:, e * 8192:(e + 1) * 8192]), "gc", 16)
            wu_gd[e] = emi("gpsimd", lambda g, e=e: g.dma_start(
                scr[:, SCR_WU + (e % 2) * 520: SCR_WU + (e % 2) * 520 + 520],
                wuaug[:, e * 520:(e + 1) * 520]), "gw", 16)

        def emit_me(e):
            wait("tensor", "gc", wct_gd[e])
            wait("tensor", "gw", wu_gd[e])
            for half in range(2):
                if half == 1 or e >= 1:
                    wait("tensor", "sc", me_sc.get(e, me_sc.get(e - 1)))
                for kc in range(C):
                    lhs = scr[:, SCR_WU + (e % 2) * 520 + kc * 65:
                              SCR_WU + (e % 2) * 520 + kc * 65 + 65]
                    rhs = wg[:, kc * 1024 + half * 512:
                             kc * 1024 + half * 512 + 512]
                    if kc == C - 1:
                        me_pe[e] = emi("tensor", lambda t, o=pb[7][0:65, 0:512],
                                       a=lhs, b=rhs:
                                       t.matmul(o, a, b, start=False, stop=True), "pe")
                    else:
                        emi("tensor", lambda t, o=pb[7][0:65, 0:512], a=lhs, b=rhs,
                            st=(kc == 0): t.matmul(o, a, b, start=st, stop=False))
                wait("scalar", "pe", me_pe[e])
                if half == 0 and e >= 1:
                    wait("scalar", "sm", me_sd[e - 1])
                me_sc[e] = emi("scalar", lambda sc, half=half: sc.copy(
                    scr[0:65, SCR_MSTG + half * 512:SCR_MSTG + half * 512 + 512],
                    pb[7][0:65, 0:512]), "sc")
            wait("sync", "sc", me_sc[e])
            me_sd[e] = emi("sync", lambda s, e=e: s.dma_start(
                mall_dram[:, e * D:(e + 1) * D],
                scr[0:65, SCR_MSTG:SCR_MSTG + 1024]), "sm", 16)
            if e + 1 < E:
                sched_wct_dma(e + 1, me_pe[e])

        def emit_me_next():
            done = set(me_pe)
            for e in range(E):
                if e in wct_gd and e not in done:
                    emit_me(e)
                    return

        sched_wct_dma(0, pe_q_done)

        wait("tensor", "ve", ve_ndbinit)
        wait("tensor", "ve", ve_eq_done)
        wait("tensor", "sc", sc_eq_done)
        ek_sc, nd_ve, ekpe, npe, k2ve = {}, {}, {}, {}, {}

        def p4_dcol(c):
            nbank = pb[2 + c % 2]
            wait("vector", "pe", npe[c])
            zv["d", c] = emi("vector", lambda v, a=nbank[:, 128:129]:
                             v.tensor_copy(dcol[:], a), "ve")

        def p4_zpair(c):
            wait("tensor", "ve", zv["d", c])
            zv["zp", c] = emi("tensor", lambda t: t.matmul(
                pb[5][0:2, 0:1], blk2_s[:], dcol[:], start=True, stop=True), "pe")

        def p4_recip(c):
            wait("vector", "pe", zv["zp", c])
            zv["iz", c] = emi("vector", lambda v: v.reciprocal(
                invz[:], pb[5][0:2, 0:1]), "ve")

        def p4_bc(c):
            wait("tensor", "ve", zv["iz", c])
            zv["bc", c] = emi("tensor", lambda t: t.matmul(
                pb[6][:, 0:1], indic2_s[:], invz[:], start=True, stop=True), "pe")

        def p4_ndb_evac(c):
            nbank = pb[2 + c % 2]
            wait("vector", "pe", zv["bc", c])
            emi("vector", lambda v, c=c, nb=nbank: v.tensor_scalar(
                ndb[0:64, c * 130:c * 130 + 64], nb[0:64, 0:64],
                pb[6][0:64, 0:1], None, ALU.mult))
            emi("vector", lambda v, c=c, nb=nbank: v.tensor_scalar(
                ndb[64:128, c * 130 + 65:c * 130 + 129], nb[64:128, 64:128],
                pb[6][64:128, 0:1], None, ALU.mult))
            emi("vector", lambda v, c=c, nb=nbank: v.tensor_scalar(
                ndb[0:64, c * 130 + 64:c * 130 + 65], nb[0:64, 128:129],
                pb[6][0:64, 0:1], EPS_ATTN, ALU.mult, ALU.add))
            nd_ve[c] = emi("vector", lambda v, c=c, nb=nbank: v.tensor_scalar(
                ndb[64:128, c * 130 + 129:c * 130 + 130], nb[64:128, 128:129],
                pb[6][64:128, 0:1], EPS_ATTN, ALU.mult, ALU.add), "ve")

        zv = {}
        k2lb = {}
        for c in range(C):
            if c - 1 in ekpe:
                wait("vector", "pe", ekpe[c - 1])
            for lb in range(4):
                src2 = axD[:, c * 2048 + lb * 512: c * 2048 + lb * 512 + 512]
                dstk = scr[:, SCR_K2 + lb * 512: SCR_K2 + lb * 512 + 512]
                k2lb[(c, lb)] = emi("scalar", lambda sc, o=dstk, a=src2:
                                    sc.activation(o, a, AF.Square, bias=0.0,
                                                  scale=1.0), "sc")
            if c >= 1:
                p4_dcol(c - 1)

            for lb in range(4):
                bank = pb[lb % 2]
                prev = ek_sc.get((c, lb - 2)) or ek_sc.get((c - 1, lb + 2))
                if prev:
                    wait("tensor", "sc", prev)
                wait("tensor", "sc", k2lb[(c, lb)])
                pe_ek = 0
                for j in range(4):
                    lt = lb * 4 + j
                    lhs1 = axD[:, c * 2048 + lt * 128: c * 2048 + lt * 128 + 128]
                    lhs2 = scr[:, SCR_K2 + lt * 128: SCR_K2 + lt * 128 + 128]
                    oc = bank[:, j * 128:j * 128 + 128]
                    emi("tensor", lambda t, o=oc, a=lhs1,
                        b=w0pb[:, c * 128:c * 128 + 128]:
                        t.matmul(o, a, b, start=True, stop=False))
                    if j == 3:
                        pe_ek = emi("tensor", lambda t, o=oc, a=lhs2, b=negmask_s[:]:
                                    t.matmul(o, a, b, start=False, stop=True), "pe")
                    else:
                        emi("tensor", lambda t, o=oc, a=lhs2, b=negmask_s[:]:
                            t.matmul(o, a, b, start=False, stop=True))
                ekpe[c] = pe_ek
                if lb == 0 and c >= 1:
                    p4_zpair(c - 1)
                    p4_recip(c - 1)
                wait("scalar", "pe", pe_ek)
                edst = scr[:, SCR_EK + (c % 2) * 2048 + lb * 512:
                           SCR_EK + (c % 2) * 2048 + lb * 512 + 512]
                ek_sc[(c, lb)] = emi("scalar", lambda sc, o=edst, i_=bank[:, 0:512]:
                                     sc.activation(o, i_, AF.Exp, bias=0.0,
                                                   scale=1.0), "sc")

            nbank = pb[2 + c % 2]
            if c - 2 in nd_ve:
                wait("tensor", "ve", nd_ve[c - 2])
            first = True
            for lb in range(4):
                wait("tensor", "sc", ek_sc[(c, lb)])
                for j in range(4):
                    lt = lb * 4 + j
                    lhs = scr[:, SCR_EK + (c % 2) * 2048 + lt * 128:
                              SCR_EK + (c % 2) * 2048 + lt * 128 + 128]
                    rhs = axC[:, lt * 1024 + c * 128: lt * 1024 + c * 128 + 128]
                    emi("tensor", lambda t, o=nbank[:, 0:128], a=lhs, b=rhs,
                        st=first: t.matmul(o, a, b, start=st, stop=False))
                    if lb == 3 and j == 3:
                        npe[c] = emi("tensor", lambda t, o=nbank[:, 128:129], a=lhs,
                                     b=onescol_s[:]:
                                     t.matmul(o, a, b, start=False, stop=True), "pe")
                    else:
                        emi("tensor", lambda t, o=nbank[:, 128:129], a=lhs,
                            b=onescol_s[:]:
                            t.matmul(o, a, b, start=False, stop=False))
                    first = False
            if c >= 1:
                p4_bc(c - 1)
                p4_ndb_evac(c - 1)
            if c in (3, 7):
                emit_me_next()
        p4_dcol(C - 1)
        p4_zpair(C - 1)
        p4_recip(C - 1)
        p4_bc(C - 1)
        p4_ndb_evac(C - 1)
        ve_nd_all = cnt["ve"]
        dbg_out("ndb", ndb[:], [("ve", ve_nd_all)])

        wait("gpsimd", "pe", ekpe[C - 1])
        gd_dw = emi("gpsimd", lambda g: g.dma_start(
            axD[:, AXD_DW:AXD_DW + 4096], downw), "ga", 16)

        wait("tensor", "gb", gd_wo)

        def xslot(lt):
            if lt % 4 < 2:
                return xstg[:, (lt % 4) * 1024:(lt % 4) * 1024 + 1024]
            return scr[:, SCR_K2 + (lt % 4 - 2) * 1024:
                       SCR_K2 + (lt % 4 - 2) * 1024 + 1024]

        XSEM = ("sx0", "sx1", "sx2", "sx3")
        xsd = {}
        for lt0 in range(4):
            xsd[lt0] = emi("gpsimd", lambda g, lt=lt0: g.dma_start(
                xslot(lt), xtok_t[lt]), XSEM[lt0 % 4], 16)

        num_sc, ctxt_ve, x2t_ve, x2_sc, wo_pe, trb_pe = {}, {}, {}, {}, {}, {}
        cw_sc = {}
        p5mk = {}
        mpend, mq_mark = [], {}
        num_grp = {}

        AGRP = ((0, 3), (3, 6), (6, 8))

        def p5_A(lt):
            for g, (c0, c1) in enumerate(AGRP):
                G = lt * 3 + g
                bank = pb[G % 3]
                if lt == 0:
                    wait("tensor", "ve", nd_ve[c1 - 1])
                if G - 3 in num_grp:
                    wait("tensor", "sc", num_grp[G - 3])
                pe_num = 0
                for c in range(c0, c1):
                    lhs = axB[:, c * 2048 + lt * 128: c * 2048 + lt * 128 + 128]
                    pe_num = emi("tensor", lambda t,
                                 o=bank[:, (c - c0) * 130:(c - c0) * 130 + 130],
                                 a=lhs, b=ndb[:, c * 130:(c + 1) * 130]:
                                 t.matmul(o, a, b, start=True, stop=True), "pe")
                if G == LT * 3 - 1:
                    p5mk["numdone"] = pe_num
                wait("vector", "pe", pe_num)
                if G - 4 in num_grp:
                    wait("vector", "sc", num_grp[G - 4])
                n = c1 - c0
                b3 = bank[:, 0:n * 130].rearrange("p (c g x) -> p c g x", c=n, g=2)
                rbase = (G % 4) * 6
                rs3 = ddbuf[:, rbase:rbase + n * 2].rearrange(
                    "p (c g) -> p c g", c=n)
                ve_r = emi("vector", lambda v, o=rs3, a=b3[:, :, :, 64]:
                           v.reciprocal(o, a), "ve")
                wait("scalar", "ve", ve_r)
                if lt - 2 in trb_pe and g == 0:
                    wait("scalar", "pe", trb_pe[lt - 2])
                for c in range(c0, c1):
                    i = lt * C + c
                    off = (c - c0) * 130
                    cbase = AXD_CTX + (lt % 2) * 1024 + c * 128
                    emi("scalar", lambda sc, o=axD[:, cbase:cbase + 64],
                        i_=bank[:, off:off + 64],
                        r=ddbuf[:, rbase + (c - c0) * 2:rbase + (c - c0) * 2 + 1]:
                        sc.activation(o, i_, AF.Identity, bias=0.0, scale=r))
                    num_sc[i] = emi("scalar", lambda sc,
                                    o=axD[:, cbase + 64:cbase + 128],
                                    i_=bank[:, off + 65:off + 129],
                                    r=ddbuf[:, rbase + (c - c0) * 2 + 1:
                                            rbase + (c - c0) * 2 + 2]:
                                    sc.activation(o, i_, AF.Identity, bias=0.0,
                                                  scale=r), "sc")
                num_grp[G] = num_sc[lt * C + c1 - 1]

        def p5_B_mm(lt):
            if lt - 1 in ctxt_ve:
                wait("tensor", "ve", ctxt_ve[lt - 1])
            for c in range(C):
                wait("tensor", "sc", num_sc[lt * C + c])
                src = axD[:, AXD_CTX + (lt % 2) * 1024 + c * 128:
                          AXD_CTX + (lt % 2) * 1024 + c * 128 + 128]
                if c == C - 1:
                    trb_pe[lt] = emi("tensor",
                                     lambda t, o=pbb[3][:, c * 128:c * 128 + 128],
                                     a=src: t.transpose(o, a, identb_s[:]), "pe")
                else:
                    emi("tensor", lambda t, o=pbb[3][:, c * 128:c * 128 + 128],
                        a=src: t.transpose(o, a, identb_s[:]))

        def p5_B_evac(lt):
            wait("vector", "pe", trb_pe[lt])
            if lt - 2 in wo_pe:
                wait("vector", "pe", wo_pe[lt - 2])
            ctxt_ve[lt] = emi("vector", lambda v,
                              o=axD[:, AXD_CTXT + (lt % 2) * 1024:
                                    AXD_CTXT + (lt % 2) * 1024 + 1024],
                              a=pbb[3][:, 0:1024]: v.tensor_copy(o, a), "ve")

        def p5_C_mm(lt):
            wait("tensor", "ve", ctxt_ve[lt])
            mark = 0
            if lt - 1 in cw_sc:
                wait("tensor", "ve", cw_sc[lt - 1])
            for oh in range(2):
                bank = pb[4 + oh]
                for kc in range(C):
                    lhs = axD[:, AXD_CTXT + (lt % 2) * 1024 + kc * 128:
                              AXD_CTXT + (lt % 2) * 1024 + kc * 128 + 128]
                    rhs = wg[:, 8192 + kc * 1024 + oh * 512:
                             8192 + kc * 1024 + oh * 512 + 512]
                    emi("tensor", lambda t, o=bank[:, 0:512], a=lhs, b=rhs,
                        st=(kc == 0): t.matmul(o, a, b, start=st, stop=False))
                mark = emi("tensor", lambda t, o=bank[:, 0:512], a=onesrow_s[:],
                           b=rows3b[0:1, 1024 + oh * 512:1024 + oh * 512 + 512]:
                           t.matmul(o, a, b, start=False, stop=True), "pe")
            wo_pe[lt] = mark

        def p5_CW(lt):
            stg = iostg[:, (lt % 2) * 1024:(lt % 2) * 1024 + 1024]
            xsl = xslot(lt)
            wait("vector", "pe", wo_pe[lt])
            wait("vector", XSEM[lt % 4], xsd[lt])
            if lt - 2 in x2_sc:
                wait("vector", "sc", x2_sc[lt - 2])
            emi("vector", lambda v, o=stg[:, 0:512], a=pb[4][:, 0:512],
                b=xsl[:, 0:512]: v.tensor_add(o, a, b))
            cw_sc[lt] = emi("vector", lambda v, o=stg[:, 512:1024],
                            a=pb[5][:, 0:512], b=xsl[:, 512:1024]:
                            v.tensor_add(o, a, b), "ve")
            if lt + 4 < LT:
                wait("gpsimd", "ve", cw_sc[lt])
                xsd[lt + 4] = emi("gpsimd", lambda g, lt2=lt + 4: g.dma_start(
                    xslot(lt2), xtok_t[lt2]), XSEM[lt % 4], 16)

        def p5_C_ln(lt):
            ls = lnst[:, (lt % 2) * 20:(lt % 2) * 20 + 20]
            nmr = ls[:, 16:17]
            rstd = ls[:, 15:16]
            stg = iostg[:, (lt % 2) * 1024:(lt % 2) * 1024 + 1024]
            wait("vector", "ve", cw_sc[lt])
            emi("vector", lambda v, o=ls[:, 0:6], a=stg[:, 0:512]:
                v.bn_stats(o, a))
            emi("vector", lambda v, o=ls[:, 6:12], a=stg[:, 512:1024]:
                v.bn_stats(o, a), "ve")
            wait("vector", "ve", cnt["ve"])
            ve_bn = emi("vector", lambda v, o=ls[:, 12:14], a=ls[:, 0:12]:
                        v.bn_aggr(o, a), "ve")
            wait("scalar", "ve", ve_bn)
            emi("scalar", lambda sc, o=ls[:, 14:15], a=ls[:, 13:14]:
                sc.activation(o, a, AF.Ln, bias=epscol[:], scale=1.0), "sc")
            wait("scalar", "sc", cnt["sc"])
            sc_r = emi("scalar", lambda sc, o=rstd, a=ls[:, 14:15]:
                       sc.activation(o, a, AF.Exp, bias=0.0, scale=-0.5), "sc")
            wait("vector", "sc", sc_r)
            ve_nmr = emi("vector", lambda v, o=nmr, a=ls[:, 12:13], b=rstd:
                         v.scalar_tensor_tensor(o, a, -1.0, b, ALU.mult, ALU.mult),
                         "ve")
            wait("scalar", "ve", ve_nmr)
            emi("scalar", lambda sc, o=axA[:, lt * 1024:lt * 1024 + 512],
                a=stg[:, 0:512], b=nmr, r=rstd:
                sc.activation(o, a, AF.Identity, bias=b, scale=r), "sc")
            x2_sc[lt] = emi("scalar", lambda sc,
                            o=axA[:, lt * 1024 + 512:lt * 1024 + 1024],
                            a=stg[:, 512:1024], b=nmr, r=rstd:
                            sc.activation(o, a, AF.Identity, bias=b, scale=r), "sc")

        def p5_D(lt):
            wait("tensor", "sc", x2_sc[lt])
            if lt - 1 in x2t_ve:
                wait("tensor", "ve", x2t_ve[lt - 1])
            for c in range(C):
                src = axA[:, lt * 1024 + c * 128: lt * 1024 + c * 128 + 128]
                if c == C - 1:
                    pe_x2 = emi("tensor",
                                lambda t, o=pbb[6][:, c * 128:c * 128 + 128],
                                a=src: t.transpose(o, a, identb_s[:]), "pe")
                else:
                    emi("tensor", lambda t, o=pbb[6][:, c * 128:c * 128 + 128],
                        a=src: t.transpose(o, a, identb_s[:]))
            wait("vector", "pe", pe_x2)
            dst3 = axC[:].rearrange("p (c l) -> p c l", c=C)[:, :,
                                                            lt * 128:lt * 128 + 128]
            src3 = pbb[6][:].rearrange("p (c l) -> p c l", c=C)
            x2t_ve[lt] = emi("vector", lambda v, o=dst3, a=src3:
                             v.tensor_copy(o, a), "ve")
            if lt % 4 == 3:
                q4 = lt // 4
                mq_mark[q4] = x2t_ve[lt]
                mpend.extend((c, q4) for c in range(C))
            for _ in range(2):
                if not mpend:
                    break
                c, q = mpend.pop(0)
                wait("vector", "ve", mq_mark[q])
                emi("vector", lambda v, c=c, q=q: v.bn_stats(
                    mstat[:, c * 24 + q * 6:c * 24 + q * 6 + 6],
                    axC[:, c * 2048 + q * 512:c * 2048 + q * 512 + 512]), "ve")

        for s in range(LT + 4):
            if s < LT:
                p5_A(s)
            if 1 <= s <= LT:
                p5_B_mm(s - 1)
                p5_B_evac(s - 1)
            if 2 <= s <= LT + 1:
                p5_C_mm(s - 2)
            if 3 <= s <= LT + 2:
                p5_C_ln(s - 3)
            if 2 <= s <= LT + 1:
                p5_CW(s - 2)
            if 4 <= s <= LT + 3:
                p5_D(s - 4)
            if s in (2, 4, 6, 8, 10, 12, 14, 16):
                emit_me_next()
        ve_x2t_done = cnt["ve"]
        sc_x2_done = cnt["sc"]
        pe_p5_done = cnt["pe"]
        dbg_out("x2b", axA[:], [("sc", sc_x2_done)])
        dbg_out("x2Tb", axC[:], [("ve", ve_x2t_done)])
        for e in range(E):
            if e not in me_pe:
                emit_me(e)

        wait("gpsimd", "pe", me_pe[E - 1])
        gd_base = emi("gpsimd", lambda g: g.dma_start(wg[:, 0:8192], wbase), "gb", 16)

        wait("sync", "pe", p5mk["numdone"])
        wait("sync", "sm", me_sd[E - 2])
        sd_mall_a = emi("sync", lambda s: s.dma_start(
            axB[0:65, AXB_MALL:AXB_MALL + 7 * D], mall_dram[:, 0:7 * D]), "sm2", 16)
        wait("sync", "sm", me_sd[E - 1])
        sd_mall = emi("sync", lambda s: s.dma_start(
            axB[0:65, AXB_MALL + 7 * D:AXB_MALL + 8 * D],
            mall_dram[:, 7 * D:8 * D]), "sm3", 16)

        wait("vector", "sc", sc_x2_done)
        wait("vector", "pe", pe_p5_done)
        for c, q in mpend:
            wait("vector", "ve", mq_mark[q])
            emi("vector", lambda v, c=c, q=q: v.bn_stats(
                mstat[:, c * 24 + q * 6:c * 24 + q * 6 + 6],
                axC[:, c * 2048 + q * 512:c * 2048 + q * 512 + 512]), "ve")
        mpend.clear()
        wait("vector", "ve", cnt["ve"])
        for c in range(C):
            emi("vector", lambda v, c=c: v.bn_aggr(
                mcols[:, c * 2:c * 2 + 2], mstat[:, c * 24:c * 24 + 24]),
                "ve" if c == C - 1 else None)
        wait("vector", "ve", cnt["ve"])
        for c in range(C):
            emi("vector", lambda v, c=c: v.tensor_mul(
                mtmp[:, c * 8:c * 8 + 1], mcols[:, c * 2:c * 2 + 1],
                mcols[:, c * 2:c * 2 + 1]), "ve" if c == C - 1 else None)
        wait("vector", "ve", cnt["ve"])
        for c in range(C):
            emi("vector", lambda v, c=c: v.tensor_add(
                mcols[:, c * 2 + 1:c * 2 + 2], mcols[:, c * 2 + 1:c * 2 + 2],
                mtmp[:, c * 8:c * 8 + 1]), "ve" if c == C - 1 else None)
        ve_m = cnt["ve"]
        dbg_out("mcols", mcols[:], [("ve", ve_m)])

        wait("tensor", "gb", gd_base)
        wait("tensor", "sc", sc_x2_done)
        wait("tensor", "ve", ve_x2t_done)
        for oh in range(2):
            bank = pb[oh]
            for kc in range(C):
                lhs = axC[:, kc * 2048: kc * 2048 + 128]
                rhs = wg[:, kc * 1024 + oh * 512: kc * 1024 + oh * 512 + 512]
                emi("tensor", lambda t, o=bank[:, 0:512], a=lhs, b=rhs,
                    st=(kc == 0): t.matmul(o, a, b, start=st, stop=False))

        wait("tensor", "ve", ve_m)
        pe_loss = 0
        for c in range(C):
            emi("tensor", lambda t, a=mcols[:, c * 2 + 1:c * 2 + 2],
                b=discw[:, c * 24:c * 24 + 8], st=(c == 0):
                t.matmul(pb[2][0:1, 0:8], a, b, start=st, stop=False))
            emi("tensor", lambda t, a=mcols[:, c * 2:c * 2 + 1],
                b=discw[:, c * 24 + 8:c * 24 + 16]:
                t.matmul(pb[2][0:1, 0:8], a, b, start=False, stop=False))
            pe_loss = emi("tensor", lambda t, b=discw[:, c * 24 + 16:c * 24 + 24],
                          sp=(c == C - 1):
                          t.matmul(pb[2][0:1, 0:8], onescolf_s[:], b,
                                   start=False, stop=sp), "pe")
        dbg_rawloss = pe_loss

        wait("vector", "pe", pe_loss)
        emi("vector", lambda v: v.tensor_reduce(lossv[0:1, 8:9], pb[2][0:1, 0:8],
                                                AX.X, ALU.min), "ve")
        wait("vector", "ve", cnt["ve"])
        ve_mask = emi("vector", lambda v: v.tensor_scalar(
            lossv[0:1, 0:8], pb[2][0:1, 0:8], lossv[0:1, 8:9], None,
            ALU.is_equal), "ve")
        dbg_out("lossv", lossv[:], [("ve", ve_mask)])
        wait("tensor", "ve", ve_mask)
        pe_mt = emi("tensor", lambda t: t.transpose(pb[5][0:8, 0:1],
                                                    lossv[0:1, 0:8],
                                                    ones1f_s[0:1, 0:1]), "pe")
        pe_mb = emi("tensor", lambda t: t.matmul(pb[6][:, 0:8], ones1f_s[:],
                                                 lossv[0:1, 0:8],
                                                 start=True, stop=True), "pe")
        wait("vector", "pe", pe_mb)
        emi("vector", lambda v: v.tensor_copy(maskbc[:], pb[6][:, 0:8]), "ve")
        ve_mbc = emi("vector", lambda v: v.tensor_copy(mask8[:], pb[5][0:8, 0:1]),
                     "ve")
        dbg_out("maskbc", maskbc[:], [("ve", ve_mbc)])

        wait("tensor", "ve", ve_mbc)
        wait("tensor", "g0", g0_all)
        emi("tensor", lambda t: t.matmul(pb[5][0:1, 0:512], mask8[:],
                                         convb_s[:, 0:512], start=True, stop=True))
        pe_bc2 = emi("tensor", lambda t: t.matmul(pb[6][0:1, 0:512], mask8[:],
                                                  convb_s[:, 512:1024],
                                                  start=True, stop=True), "pe")

        wait("vector", "ve", ve_mbc)
        mid128 = scr
        for e in range(E):
            emi("vector", lambda v, e=e: v.tensor_scalar(
                mid128[:, e * 128:e * 128 + 128], identb_s[:],
                maskbc[:, e:e + 1], None, ALU.mult),
                "ve" if e == E - 1 else None)
        ve_mid = cnt["ve"]
        emi("vector", lambda v: v.tensor_mul(
            mtmp[0:64, 0:8], smallsb[0:64, 144:152], maskbc[0:64, 0:8]), "ve")
        wait("vector", "ve", cnt["ve"])
        ve_bd = emi("vector", lambda v: v.tensor_reduce(
            bdcol[:], mtmp[0:64, 0:8], AX.X, ALU.add), "ve")

        wait("tensor", "ga", gd_dw)
        wait("tensor", "ve", ve_mid)
        wait("tensor", "sc", sc_x2_done)
        for c in range(C):
            for e in range(E):
                dwe = axD[:, AXD_DW + c * 512 + e * 64:AXD_DW + c * 512 + e * 64 + 64]
                emi("tensor", lambda t, o=pb[3][:, c * 64:c * 64 + 64],
                    a=mid128[:, e * 128:e * 128 + 128], b=dwe,
                    st=(e == 0), sp=(e == E - 1):
                    t.matmul(o, a, b, start=st, stop=sp),
                    "pe" if (c == C - 1 and e == E - 1) else None)
        pe_wd = cnt["pe"]
        wait("tensor", "sm2", sd_mall_a)
        msel_pe = {}
        wait("tensor", "sc", me_sc[E - 1])
        for oh in range(2):
            bank = pb[4] if oh == 0 else pb[7]
            for e in range(E):
                if oh == 0 and e == E - 1:
                    wait("tensor", "sm3", sd_mall)
                mall_e = axB[0:65, AXB_MALL + e * 1024 + oh * 512:
                             AXB_MALL + e * 1024 + oh * 512 + 512]
                emi("tensor", lambda t, o=bank[0:65, 0:512],
                    a=mid128[0:65, e * 128:e * 128 + 65], b=mall_e,
                    st=(e == 0), sp=(e == E - 1):
                    t.matmul(o, a, b, start=st, stop=sp),
                    "pe" if e == E - 1 else None)
            msel_pe[oh] = cnt["pe"]
        wait("scalar", "pe", pe_wd)
        sc_wd = emi("scalar", lambda sc: sc.activation(
            axB[:, AXB_WDSEL:AXB_WDSEL + 512], pb[3][:, 0:512],
            AF.Identity, bias=0.0, scale=1.0), "sc")
        wait("scalar", "pe", msel_pe[1])
        emi("scalar", lambda sc: sc.activation(
            axB[0:65, AXB_MSEL:AXB_MSEL + 512], pb[4][0:65, 0:512],
            AF.Identity, bias=0.0, scale=1.0), "sc")
        sc_msel = emi("scalar", lambda sc: sc.activation(
            axB[0:65, AXB_MSEL + 512:AXB_MSEL + 1024], pb[7][0:65, 0:512],
            AF.Identity, bias=0.0, scale=1.0), "sc")
        dbg_out("wdsel", axB[:, AXB_WDSEL:AXB_WDSEL + 512], [("sc", sc_wd)])

        rowstg = axB[0:1, AXB_ROW:AXB_ROW + 1024]
        wait("vector", "pe", pe_bc2)
        emi("vector", lambda v: v.tensor_add(rowstg[0:1, 0:512], pb[5][0:1, 0:512],
                                             rows3b[0:1, 2048:2560]))
        ve_br = emi("vector", lambda v: v.tensor_add(rowstg[0:1, 512:1024],
                                             pb[6][0:1, 0:512],
                                             rows3b[0:1, 2560:3072]), "ve")
        ve_hone = emi("vector", lambda v: v.memset(
            axB[64:65, AXB_H:AXB_H + 2048], 1.0), "ve")
        dbg_out("msel", axB[0:65, AXB_MSEL:AXB_MSEL + 1024], [("ve", ve_br)])

        wait("tensor", "sc", sc_wd)
        wait("tensor", "sc", me_sc[E - 1])
        wait("scalar", "ve", ve_bd)
        h_sc = {}
        for ltb in range(4):
            if ltb - 1 in h_sc:
                wait("tensor", "sc", h_sc[ltb - 1])
            for kc in range(C):
                lhs = axB[:, AXB_WDSEL + kc * 64:AXB_WDSEL + kc * 64 + 64]
                rhs = axC[:, kc * 2048 + ltb * 512: kc * 2048 + ltb * 512 + 512]
                if kc == C - 1:
                    pe_h = emi("tensor", lambda t, o=pb[3][0:64, 0:512], a=lhs, b=rhs:
                               t.matmul(o, a, b, start=False, stop=True), "pe")
                else:
                    emi("tensor", lambda t, o=pb[3][0:64, 0:512], a=lhs, b=rhs,
                        st=(kc == 0): t.matmul(o, a, b, start=st, stop=False))
            wait("scalar", "pe", pe_h)
            h_sc[ltb] = emi("scalar", lambda sc, ltb=ltb: sc.activation(
                axB[0:64, AXB_H + ltb * 512:AXB_H + ltb * 512 + 512],
                pb[3][0:64, 0:512], AF.Relu, bias=bdcol[:], scale=1.0), "sc")
        sc_h_done = cnt["sc"]
        dbg_out("hbuf", axB[0:65, AXB_H:AXB_H + 2048], [("sc", sc_h_done)])

        wait("tensor", "gb", gd_base)
        wait("tensor", "sc", sc_msel)
        wait("tensor", "ve", ve_br)
        wait("tensor", "ve", ve_hone)
        wait("tensor", "sc", sc_h_done)
        ffn_pe, o_sc, o_sd = {}, {}, {}
        for lt in range(LT):
            b0 = pb[0 + 2 * (lt % 2)]
            b1 = pb[1 + 2 * (lt % 2)]
            ls = lnst[:, (lt % 2) * 20:(lt % 2) * 20 + 20]
            nmr = ls[:, 16:17]
            rstd = ls[:, 15:16]
            if lt - 2 in o_sc:
                wait("tensor", "sc", o_sc[lt - 2])
            for oh in range(2):
                bank = b0 if oh == 0 else b1
                if lt > 0:
                    for kc in range(C):
                        lhs = axC[:, kc * 2048 + lt * 128:
                                  kc * 2048 + lt * 128 + 128]
                        rhs = wg[:, kc * 1024 + oh * 512:
                                 kc * 1024 + oh * 512 + 512]
                        emi("tensor", lambda t, o=bank[:, 0:512], a=lhs, b=rhs,
                            st=(kc == 0): t.matmul(o, a, b, start=st, stop=False))
                emi("tensor", lambda t, o=bank[:, 0:512], a=onesrow_s[:],
                    b=rowstg[0:1, oh * 512:oh * 512 + 512]:
                    t.matmul(o, a, b, start=False, stop=False))
                ffn_pe[(lt, oh)] = emi("tensor", lambda t, o=bank[:, 0:512],
                                       a=axB[0:65, AXB_H + lt * 128:
                                             AXB_H + lt * 128 + 128],
                                       b=axB[0:65, AXB_MSEL + oh * 512:
                                             AXB_MSEL + oh * 512 + 512]:
                                       t.matmul(o, a, b, start=False, stop=True), "pe")
            wait("vector", "pe", ffn_pe[(lt, 1)])
            emi("vector", lambda v, o=b0[:, 0:512], a=b0[:, 0:512],
                b=axA[:, lt * 1024:lt * 1024 + 512]: v.tensor_add(o, a, b))
            emi("vector", lambda v, o=b1[:, 0:512], a=b1[:, 0:512],
                b=axA[:, lt * 1024 + 512:lt * 1024 + 1024]: v.tensor_add(o, a, b),
                "ve")
            wait("vector", "ve", cnt["ve"])
            emi("vector", lambda v, o=ls[:, 0:6], a=b0[:, 0:512]: v.bn_stats(o, a))
            emi("vector", lambda v, o=ls[:, 6:12], a=b1[:, 0:512]: v.bn_stats(o, a),
                "ve")
            wait("vector", "ve", cnt["ve"])
            ve_bn = emi("vector", lambda v, o=ls[:, 12:14], a=ls[:, 0:12]:
                        v.bn_aggr(o, a), "ve")
            wait("scalar", "ve", ve_bn)
            emi("scalar", lambda sc, o=ls[:, 14:15], a=ls[:, 13:14]:
                sc.activation(o, a, AF.Ln, bias=epscol[:], scale=1.0), "sc")
            wait("scalar", "sc", cnt["sc"])
            sc_r = emi("scalar", lambda sc, o=rstd, a=ls[:, 14:15]:
                       sc.activation(o, a, AF.Exp, bias=0.0, scale=-0.5), "sc")
            wait("vector", "sc", sc_r)
            ve_nmr = emi("vector", lambda v, o=nmr, a=ls[:, 12:13], b=rstd:
                         v.scalar_tensor_tensor(o, a, -1.0, b, ALU.mult, ALU.mult),
                         "ve")
            wait("scalar", "ve", ve_nmr)
            if lt - 2 in o_sd:
                wait("scalar", "so0" if lt % 2 == 0 else "so1", o_sd[lt - 2])
            emi("scalar", lambda sc, o=iostg[:, (lt % 2) * 1024:(lt % 2) * 1024 + 512],
                a=b0[:, 0:512], b=nmr, r=rstd:
                sc.activation(o, a, AF.Identity, bias=b, scale=r))
            o_sc[lt] = emi("scalar", lambda sc,
                           o=iostg[:, (lt % 2) * 1024 + 512:(lt % 2) * 1024 + 1024],
                           a=b1[:, 0:512], b=nmr, r=rstd:
                           sc.activation(o, a, AF.Identity, bias=b, scale=r), "sc")
            wait("sync", "sc", o_sc[lt])
            o_sd[lt] = emi("sync", lambda s, lt=lt: s.dma_start(
                out_t[lt], iostg[:, (lt % 2) * 1024:(lt % 2) * 1024 + 1024]),
                "so0" if lt % 2 == 0 else "so1", 16)

        wait("sync", "so0", cnt["so0"])
        wait("sync", "so1", cnt["so1"])
        if cnt["sd"]:
            wait("sync", "sd", cnt["sd"])

        blk = ctx.enter_context(nc.Block())

        @blk.tensor
        def _(t):
            for f in opsq["tensor"]:
                f(t)

        @blk.scalar
        def _(s):
            for f in opsq["scalar"]:
                f(s)

        @blk.vector
        def _(v):
            for f in opsq["vector"]:
                f(v)

        @blk.sync
        def _(s):
            for f in opsq["sync"]:
                f(s)

        @blk.gpsimd
        def _(g):
            for f in opsq["gpsimd"]:
                f(g)

    return nc


def _chunkT(wT):
    m = wT.shape[1]
    return np.ascontiguousarray(
        wT.reshape(8, 128, m).transpose(1, 0, 2).reshape(128, 8 * m))


def prep_inputs(inp):
    f32 = np.float32
    Wq = np.asarray(inp["L0_attn_Wq"], f32)
    Wk = np.asarray(inp["L0_attn_Wk"], f32)
    Wv = np.asarray(inp["L0_attn_Wv"], f32)
    Wo = np.asarray(inp["L0_attn_Wo"], f32)
    Wrf = np.asarray(inp["L0_attn_Wrf"], f32)
    baseW = np.asarray(inp["L0_ffn_base_W"], f32)
    x = np.asarray(inp["L0_attn_x"], f32)

    shared = {}
    shared["wq"] = _chunkT(Wq.T)
    shared["wk"] = np.ascontiguousarray(
        Wk.T.reshape(8, 128, 8, 128).transpose(1, 2, 0, 3).reshape(128, 8192))
    shared["wv"] = _chunkT(Wv.T)
    shared["wo"] = _chunkT(Wo.T)
    shared["wbase"] = _chunkT(baseW.T)

    W0 = Wrf[0]
    blk = np.zeros((8, 128, 128), f32)
    for c in range(8):
        blk[c, 0:64, 0:64] = W0[2 * c]
        blk[c, 64:128, 64:128] = W0[2 * c + 1]
    shared["w0pair"] = np.ascontiguousarray(
        blk.transpose(1, 0, 2).reshape(128, 1024))

    downW = np.asarray(inp["L0_ffn_down_W"], f32)
    shared["downw"] = np.ascontiguousarray(
        downW.reshape(8, 8, 128, 64).transpose(2, 1, 0, 3).reshape(128, 4096))

    upW = np.asarray(inp["L0_ffn_up_W"], f32)
    upb = np.asarray(inp["L0_ffn_up_b"], f32)
    aug = np.concatenate([upW.transpose(0, 2, 1),
                          upb[:, :, None]], axis=2)
    shared["wuaug"] = np.ascontiguousarray(
        aug.reshape(8, 8, 128, 65).transpose(2, 0, 1, 3).reshape(128, 8 * 8 * 65))

    convW = np.asarray(inp["L0_ffn_conv_W"], f32)
    shared["wct"] = np.ascontiguousarray(
        convW.transpose(0, 2, 1).reshape(8, 8, 128, 1024)
        .transpose(2, 0, 1, 3).reshape(128, 8 * 8192))

    bq = np.asarray(inp["L0_attn_bq"], f32)
    bk = np.asarray(inp["L0_attn_bk"], f32)
    mu = np.asarray(inp["L0_ffn_disc_mu"], f32)
    sg = np.asarray(inp["L0_ffn_disc_sigma"], f32)
    bd = np.asarray(inp["L0_ffn_down_b"], f32)
    smalls = np.zeros((128, 152), f32)
    smalls[:, 0:8] = bq.reshape(8, 128).T
    smalls[:, 8:16] = bk.reshape(8, 128).T
    smalls[:, 16:80] = mu.T.reshape(8, 128, 8).transpose(1, 0, 2).reshape(128, 64)
    smalls[:, 80:144] = sg.T.reshape(8, 128, 8).transpose(1, 0, 2).reshape(128, 64)
    smalls[0:64, 144:152] = bd.T
    shared["smalls"] = smalls

    bv = np.asarray(inp["L0_attn_bv"], f32)
    bo = np.asarray(inp["L0_attn_bo"], f32)
    bb = np.asarray(inp["L0_ffn_base_b"], f32)
    shared["rows3"] = np.concatenate([bv, bo, bb])[None, :].astype(f32)
    shared["convb"] = np.asarray(inp["L0_ffn_conv_b"], f32)

    shared["identb"] = np.eye(128, dtype=f32)
    nm = np.zeros((128, 128), f32)
    nm[0:64, 0:64] = -0.5
    nm[64:128, 64:128] = -0.5
    shared["negmask"] = nm
    shared["onesrow_b"] = np.ones((1, 128), f32)
    shared["onescol_b"] = np.ones((128, 1), f32)
    shared["ones1f"] = np.ones((1, 128), f32)
    shared["onescolf"] = np.ones((128, 1), f32)
    b2 = np.zeros((128, 2), f32)
    b2[0:64, 0] = 1.0
    b2[64:128, 1] = 1.0
    shared["blockones2"] = b2
    i2 = np.zeros((2, 128), f32)
    i2[0, 0:64] = 1.0
    i2[1, 64:128] = 1.0
    shared["indic2"] = i2

    in_maps = []
    for b in range(NCORES):
        m = dict(shared)
        m["xT"] = np.ascontiguousarray(
            x[b].T.reshape(8, 128, 4, 512).transpose(1, 2, 0, 3).reshape(128, 16384))
        m["xtok"] = np.ascontiguousarray(x[b])
        in_maps.append(m)
    return in_maps


def kernel(**inputs):
    in_maps = prep_inputs(inputs)
    nc = build_program(DBG)
    res = run_bass_kernel_spmd(nc, in_maps, core_ids=list(range(NCORES)))
    return np.stack([np.asarray(r["out"], np.float32) for r in res.results], axis=0)
